# version 03
# baseline (speedup 1.0000x reference)
import numpy as np
import contextlib
import concourse.bass as bass
import concourse.mybir as mybir
from concourse.bass_utils import run_bass_kernel_spmd

F32 = mybir.dt.float32
BF16 = mybir.dt.bfloat16
AF = mybir.ActivationFunctionType
ALU = mybir.AluOpType
AX = mybir.AxisListType
ENGINES = ("sync", "scalar", "vector", "gpsimd", "tensor")
DEBUG_L0B = False

D = 1024
T = 4096
NK = 8
NH = 16
HD = 64
NTOK = 8
CW = 512


class Builder:
    DMA_POOL = {"sync": 20, "gpsimd": 12, "scalar": 6, "vector": 6}

    def __init__(self, nc, stack):
        self.nc = nc
        self.stack = stack
        self.ops = {e: [] for e in ENGINES}
        self.nsem = 0
        self.cmp_sem = {e: self.new_sem() for e in ("scalar", "vector", "gpsimd", "tensor")}
        self.dma_pool = {e: [self.new_sem() for _ in range(n)]
                         for e, n in self.DMA_POOL.items()}
        self.dma_idx = {e: 0 for e in self.DMA_POOL}
        self.cnt = {}
        for s in list(self.cmp_sem.values()):
            self.cnt[s] = 0
        for e in self.dma_pool:
            for s in self.dma_pool[e]:
                self.cnt[s] = 0
        self.dma_sems = set()
        for e in self.dma_pool:
            self.dma_sems.update(self.dma_pool[e])
        self.waited = {e: {} for e in ENGINES}
        self.nbuf = 0

    def new_sem(self):
        s = self.stack.enter_context(self.nc.semaphore(f"s{self.nsem}"))
        self.nsem += 1
        return s

    def sb(self, shape, dtype, stack=None):
        self.nbuf += 1
        return (stack or self.stack).enter_context(
            self.nc.sbuf_tensor(f"t{self.nbuf}", shape, dtype))

    def ps(self, shape, dtype=F32, stack=None):
        self.nbuf += 1
        return (stack or self.stack).enter_context(
            self.nc.psum_tensor(f"p{self.nbuf}", shape, dtype))

    def emit(self, eng, build, waits=(), dma=False, sig=True):
        self_wait = None
        if dma:
            pool = self.dma_pool[eng]
            sem = pool[self.dma_idx[eng] % len(pool)]
            self.dma_idx[eng] += 1
            step = 16
            if self.cnt[sem] > 0:
                self_wait = (sem, self.cnt[sem])
        else:
            sem = self.cmp_sem[eng]
            step = 1
            if eng in ("vector", "scalar", "gpsimd") and self.cnt[sem] > 0:
                self_wait = (sem, self.cnt[sem])
        tok = None
        if sig:
            self.cnt[sem] += step
            tok = (sem, self.cnt[sem])
        own_cmp = self.cmp_sem.get(eng)
        wl = []
        wd = self.waited[eng]
        flat = []
        for w in waits:
            if w is None:
                continue
            if isinstance(w, list):
                flat.extend(x for x in w if x is not None)
            else:
                flat.append(w)
        if self_wait is not None:
            s, v = self_wait
            if wd.get(s, -1) < v:
                wd[s] = v
                wl.append((s, v))
        for w in flat:
            s, v = w
            if s is own_cmp:
                continue
            if wd.get(s, -1) >= v:
                continue
            if s in self.dma_sems and v < self.cnt[s]:
                v = self.cnt[s]
                if wd.get(s, -1) >= v:
                    continue
            wd[s] = v
            wl.append((s, v))
        def run(h, _wl=wl, _build=build, _tok=tok, _step=step, _sem=sem, _sig=sig):
            for (s, v) in _wl:
                h.wait_ge(s, v)
            ins = _build(h)
            if _sig:
                ins.then_inc(_sem, _step)
        self.ops[eng].append(run)
        return tok

    def raw(self, eng, fn):
        self.ops[eng].append(fn)

    def tok(self, eng, dma=False):
        if dma:
            return [(s, self.cnt[s]) for s in self.dma_pool[eng] if self.cnt[s] > 0]
        sem = self.cmp_sem[eng]
        return (sem, self.cnt[sem])

    def barrier(self):
        toks = [(s, self.cnt[s]) for s in self.cnt if self.cnt[s] > 0]
        for e in ENGINES:
            own = self.cmp_sem.get(e)
            wl = []
            for (s, v) in toks:
                if s is own:
                    continue
                if self.waited[e].get(s, -1) >= v:
                    continue
                self.waited[e][s] = v
                wl.append((s, v))
            if wl:
                def run(h, _wl=tuple(wl)):
                    for (s, v) in _wl:
                        h.wait_ge(s, v)
                self.ops[e].append(run)

    def finalize(self, final_toks):
        flat = []
        for t in final_toks:
            if t is None:
                continue
            if isinstance(t, list):
                flat.extend(x for x in t if x is not None)
            else:
                flat.append(t)
        for t in flat:
            s, v = t
            self.raw("gpsimd", lambda h, _s=s, _v=v: h.wait_ge(_s, _v))
        with self.nc.Block() as blk:
            for e in ENGINES:
                lst = self.ops[e]
                if not lst:
                    continue
                def runner(h, _lst=lst):
                    for fn in _lst:
                        fn(h)
                getattr(blk, e)(runner)


def mm_chain(b, out_ap, pairs, waits=(), free_tok=None, first=True, last=True):
    n = len(pairs)
    tok = None
    for i, (lt, rh) in enumerate(pairs):
        w = list(waits) + [free_tok] if i == 0 else ()
        tok = b.emit("tensor",
                     lambda h, lt=lt, rh=rh, s=(first and i == 0), e=(last and i == n - 1):
                     h.matmul(out=out_ap, lhsT=lt, rhs=rh, start=s, stop=e),
                     waits=w, sig=(i == n - 1))
    return tok


def cast_resident(b, st, hbm, shape_tiles, wdt=BF16):
    tiles = []
    toks = []
    for k in range(NK):
        f32t = b.sb([128, 1024], F32, stack=st)
        bft = b.sb([128, 1024], wdt, stack=st)
        lt = b.emit("sync", lambda h, d=f32t, s=hbm[k * 128:(k + 1) * 128, :]:
                    h.dma_start(out=d[:], in_=s), dma=True)
        ct = b.emit("vector", lambda h, d=bft, s=f32t: h.tensor_copy(out=d[:], in_=s[:]),
                    waits=[lt])
        tiles.append(bft)
        toks.append(ct)
    return tiles, toks


def load_col(b, st, hbm_slice, w=1):
    t = b.sb([128, w], F32, stack=st)
    tok = b.emit("sync", lambda h, d=t, s=hbm_slice: h.dma_start(out=d[:], in_=s), dma=True)
    return t, tok


def ln_chunk(b, st, banks, tiles, W, ones_col, eps_col, out_tiles, out_bf=None,
             affine=None, waits=(), sq=None, rows=None):
    bs, bs_free = banks.get()
    bq, bq_free = banks.get()
    bm, bm_free = banks.get()
    br, br_free = banks.get()
    t_prev_q = None
    for m in range(NK):
        sqt_ = b.emit("scalar", lambda h, s=tiles[m], d=sq:
                      h.activation(out=d[:, 0:W], in_=s, func=AF.Square),
                      waits=list(waits) + [t_prev_q])
        t_prev_q = mm_chain(b, bq[0:1, 0:W], [(ones_col[0:128, 0:1], sq[:, 0:W])],
                            waits=[sqt_], free_tok=bq_free if m == 0 else None,
                            first=(m == 0), last=(m == NK - 1))
        t_s = mm_chain(b, bs[0:1, 0:W], [(ones_col[0:128, 0:1], tiles[m])],
                       waits=list(waits), free_tok=bs_free if m == 0 else None,
                       first=(m == 0), last=(m == NK - 1))
    t_q = b.tok("tensor")
    mu = rows[0:1, 0:W]; ex2 = rows[0:1, W:2 * W]
    var = rows[0:1, 2 * W:3 * W]; rstd = rows[0:1, 3 * W:4 * W]
    t1 = b.emit("vector", lambda h: h.tensor_scalar(out=mu, in0=bs[0:1, 0:W], scalar1=1.0 / D,
                scalar2=None, op0=ALU.mult), waits=[t_q])
    t2 = b.emit("vector", lambda h: h.tensor_scalar(out=ex2, in0=bq[0:1, 0:W], scalar1=1.0 / D,
                scalar2=None, op0=ALU.mult))
    t3 = b.emit("vector", lambda h: h.tensor_tensor(out=var, in0=mu, in1=mu, op=ALU.mult))
    t4 = b.emit("vector", lambda h: h.tensor_tensor(out=var, in0=ex2, in1=var, op=ALU.subtract))
    t5 = b.emit("scalar", lambda h: h.activation(out=var, in_=var, func=AF.Sqrt,
                bias=eps_col[0:1, 0:1]), waits=[t4])
    t6 = b.emit("vector", lambda h: h.reciprocal(out=rstd, in_=var), waits=[t5])
    t_bm = mm_chain(b, bm[:, 0:W], [(ones_col[0:1, 0:128], mu)], waits=[t1], free_tok=bm_free)
    t_br = mm_chain(b, br[:, 0:W], [(ones_col[0:1, 0:128], rstd)], waits=[t6], free_tok=br_free)
    vt = None
    for m in range(NK):
        b.emit("vector", lambda h, i=m: h.tensor_tensor(out=out_tiles[i], in0=tiles[i],
               in1=bm[:, 0:W], op=ALU.subtract), waits=[t_bm])
        if affine is not None:
            b.emit("vector", lambda h, i=m: h.tensor_tensor(out=out_tiles[i], in0=out_tiles[i],
                   in1=br[:, 0:W], op=ALU.mult), waits=[t_br])
            vt = b.emit("vector", lambda h, i=m: h.tensor_scalar(out=out_tiles[i],
                        in0=out_tiles[i], scalar1=affine[0][i][:], scalar2=affine[1][i][:],
                        op0=ALU.mult, op1=ALU.add))
        else:
            vt = b.emit("vector", lambda h, i=m: h.tensor_tensor(out=out_tiles[i],
                        in0=out_tiles[i], in1=br[:, 0:W], op=ALU.mult), waits=[t_br])
        if out_bf is not None:
            vt = b.emit("vector", lambda h, i=m: h.tensor_copy(out=out_bf[i], in_=out_tiles[i]))
    banks.set_free(bs, vt); banks.set_free(bq, vt); banks.set_free(bm, vt); banks.set_free(br, vt)
    return vt


class Banks:
    def __init__(self, b, tiles):
        self.b = b
        self.t = list(tiles)
        self.free = {id(t): None for t in self.t}
        self.i = 0

    def get(self):
        t = self.t[self.i % len(self.t)]
        self.i += 1
        return t, self.free[id(t)]

    def set_free(self, t, tok):
        self.free[id(t)] = tok

    def reset(self):
        for k in self.free:
            self.free[k] = None


def cast_resident2(b, st, hbm, wdt=BF16):
    stg = [b.sb([128, 1024], F32, stack=st) for _ in range(2)]
    stg_free = [None, None]
    tiles = []
    tok = None
    for k in range(NK):
        s = k % 2
        lt = b.emit("sync", lambda h, d=stg[s], src=hbm[k * 128:(k + 1) * 128, :]:
                    h.dma_start(out=d[:], in_=src), dma=True,
                    waits=[stg_free[s]])
        bft = b.sb([128, 1024], wdt, stack=st)
        tok = b.emit("vector", lambda h, d=bft, s2=stg[s]: h.tensor_copy(out=d[:], in_=s2[:]),
                     waits=[lt])
        stg_free[s] = tok
        tiles.append(bft)
    return tiles, tok


def stage_l0_attn(b, tn, banks):
    nc = b.nc
    st = contextlib.ExitStack()
    ek_h = tn("ek_h", [D, T], F32, "Internal")
    if DEBUG_L0B:
        dbg = {i: tn(f"dbg{i}", [D, T], F32, "Internal") for i in range(1, 6)}
    ekv_h = tn("ekv_h", [D, T], F32, "Internal")
    sr_h = tn("sr_h", [D, T], F32, "Internal")
    yb_h = tn("yb_h", [D, T], BF16, "Internal")
    xT = tn("xT", [D, T], F32, "in")
    x1T = tn("x1T", [D, T], F32, "out_x1")

    skipA = (DEBUG_L0B == 3)
    stA = contextlib.ExitStack()
    if not skipA:
        wr, twr = cast_resident2(b, stA, tn("l0wr", [D, D], F32, "in"))
        wk, twk = cast_resident2(b, stA, tn("l0wk", [D, D], F32, "in"))
        wv, twv = cast_resident2(b, stA, tn("l0wv", [D, D], F32, "in"))
    mixc = {}
    for nm in (() if skipA else ("tmr", "tmk", "tmv")):
        hbmc = tn("l0" + nm, [NK, 128, 1], F32, "in")
        cols = []
        for k in range(NK):
            c, tc = load_col(b, stA, hbmc[k])
            c1 = b.sb([128, 1], F32, stack=stA)
            b.emit("vector", lambda h, d=c1, s=c: h.tensor_scalar(out=d[:], in0=s[:],
                   scalar1=-1.0, scalar2=1.0, op0=ALU.mult, op1=ALU.add), waits=[tc])
            cols.append((c, c1))
        mixc[nm] = cols
    xs = [b.sb([128, CW + 1], F32, stack=stA) for _ in range(NK)]
    mx = {nm: [b.sb([128, CW], BF16, stack=stA) for _ in range(NK)] for nm in ("r", "k", "v")}
    vsb = [b.sb([128, CW], F32, stack=stA) for _ in range(NK)]
    eksb = b.sb([128, CW], F32, stack=stA)
    tmpV = b.sb([128, CW], F32, stack=stA)
    tmpG = b.sb([128, CW], F32, stack=stA)
    x_free = []
    ek_war = None
    for n in range(0 if skipA else NTOK):
        lts = []
        for k in range(NK):
            if n == 0:
                lt = b.emit("sync", lambda h, d=xs[k][:, 1:CW + 1],
                            s=xT[k * 128:(k + 1) * 128, 0:CW]: h.dma_start(out=d, in_=s),
                            dma=True, waits=x_free)
                lt = b.emit("vector", lambda h, d=xs[k][:, 0:1]: h.memset(d, 0.0), waits=[lt])
            else:
                lt = b.emit("sync", lambda h, d=xs[k][:, 0:CW + 1],
                            s=xT[k * 128:(k + 1) * 128, n * CW - 1:(n + 1) * CW]:
                            h.dma_start(out=d, in_=s), dma=True, waits=x_free)
            lts.append(lt)
        for nm, key in (("tmr", "r"), ("tmk", "k"), ("tmv", "v")):
            for k in range(NK):
                c, c1 = mixc[nm][k]
                eng = "vector"
                tmp = tmpV
                b.emit(eng, lambda h, k=k, c=c, tmp=tmp: h.tensor_scalar(out=tmp[:],
                       in0=xs[k][:, 1:CW + 1],
                       scalar1=c[:], scalar2=None, op0=ALU.mult), waits=[lts[k]])
                b.emit(eng, lambda h, k=k, key=key, c1=c1, tmp=tmp: h.scalar_tensor_tensor(
                    out=mx[key][k][:], in0=xs[k][:, 0:CW], scalar=c1[:], in1=tmp[:],
                    op0=ALU.mult, op1=ALU.add))
        tv = b.tok("vector"); tg = b.tok("gpsimd")
        ev_toks = []
        for m in range(NK):
            bank, bfree = banks.get()
            mmt = mm_chain(b, bank[:, 0:CW], [(wv[k][:, m * 128:(m + 1) * 128], mx["v"][k][:])
                                               for k in range(NK)],
                           waits=[tv, tg, twv], free_tok=bfree)
            et = b.emit("vector", lambda h, m=m, bank=bank: h.tensor_copy(out=vsb[m][:],
                        in_=bank[:, 0:CW]), waits=[mmt])
            banks.set_free(bank, et)
            ev_toks.append(et)
        for m in range(NK):
            bank, bfree = banks.get()
            mmt = mm_chain(b, bank[:, 0:CW], [(wk[k][:, m * 128:(m + 1) * 128], mx["k"][k][:])
                                               for k in range(NK)],
                           waits=[twk], free_tok=bfree)
            t1 = b.emit("scalar", lambda h, bank=bank: h.activation(out=eksb[:],
                        in_=bank[:, 0:CW], func=AF.Exp), waits=[mmt, ek_war])
            banks.set_free(bank, t1)
            t2 = b.emit("gpsimd", lambda h, m=m, n=n: h.dma_start(
                out=ek_h[m * 128:(m + 1) * 128, n * CW:(n + 1) * CW], in_=eksb[:]),
                dma=True, waits=[t1])
            t3 = b.emit("vector", lambda h, m=m: h.tensor_tensor(out=vsb[m][:], in0=eksb[:],
                        in1=vsb[m][:], op=ALU.mult), waits=[t1, ev_toks[m]])
            t4 = b.emit("gpsimd", lambda h, m=m, n=n: h.dma_start(
                out=ekv_h[m * 128:(m + 1) * 128, n * CW:(n + 1) * CW], in_=vsb[m][:]),
                dma=True, waits=[t3])
            ek_war = t4
        for m in range(NK):
            bank, bfree = banks.get()
            mmt = mm_chain(b, bank[:, 0:CW], [(wr[k][:, m * 128:(m + 1) * 128], mx["r"][k][:])
                                               for k in range(NK)],
                           waits=[twr], free_tok=bfree)
            t1 = b.emit("scalar", lambda h, bank=bank, m=m: h.activation(out=vsb[m][:],
                        in_=bank[:, 0:CW], func=AF.Sigmoid), waits=[mmt, b.tok("gpsimd", dma=True)])
            banks.set_free(bank, t1)
            b.emit("gpsimd", lambda h, m=m, n=n: h.dma_start(
                out=sr_h[m * 128:(m + 1) * 128, n * CW:(n + 1) * CW], in_=vsb[m][:]),
                dma=True, waits=[t1])
        x_free = [tv, tg, b.tok("tensor"), b.tok("gpsimd", dma=True)]
    stA.close()
    b.barrier()
    banks.reset()

    stB = contextlib.ExitStack()
    B0 = b.sb([128, T], F32, stack=stB)
    B1 = b.sb([128, T], F32, stack=stB)
    B2 = b.sb([128, T], F32, stack=stB)
    B3 = b.sb([128, T], F32, stack=stB)
    B4 = b.sb([128, T], F32, stack=stB)
    B5 = b.sb([128, T], F32, stack=stB)
    ybt = b.sb([128, T], BF16, stack=stB)
    bo2 = b.sb([128, 2], F32, stack=stB)
    b.emit("vector", lambda h: h.memset(bo2[:], 0.0))
    b.emit("vector", lambda h: h.memset(bo2[0:64, 0:1], 1.0))
    t_bo2 = b.emit("vector", lambda h: h.memset(bo2[64:128, 1:2], 1.0))
    crep2 = b.sb([2, 128], F32, stack=stB)
    t_crep2 = b.emit("sync", lambda h: h.dma_start(out=crep2[:], in_=tn("c_rep2", [2, 128], F32, "in")),
                     dma=True)
    tdec_h = tn("l0dec", [NK, 128, 1], F32, "in")
    tfir_h = tn("l0fir", [NK, 128, 1], F32, "in")
    gnw_h = tn("l0gnw", [NK, 128, 1], F32, "in")
    gnb_h = tn("l0gnb", [NK, 128, 1], F32, "in")
    cols4 = b.sb([128, 4], F32, stack=stB)
    wcol = b.sb([128, 1], F32, stack=stB)
    eucol = b.sb([128, 1], F32, stack=stB)
    acol = b.sb([128, 1], F32, stack=stB)
    bcol = b.sb([128, 1], F32, stack=stB)
    mv = b.sb([2, 2], F32, stack=stB)
    epsB = b.sb([128, 1], F32, stack=stB)
    b.emit("vector", lambda h: h.memset(epsB[:], 1e-5))
    rows2 = b.sb([2, 6], F32, stack=stB)
    murs = b.sb([128, 2], F32, stack=stB)
    prev = None
    prev_vec_guard = None
    for m in range(NK):
        l1 = b.emit("sync", lambda h, m=m: h.dma_start(out=cols4[:, 0:1], in_=tdec_h[m]),
                    dma=True, waits=[prev])
        l1 = b.emit("sync", lambda h, m=m: h.dma_start(out=cols4[:, 1:2], in_=tfir_h[m]), dma=True)
        l1 = b.emit("sync", lambda h, m=m: h.dma_start(out=cols4[:, 2:3], in_=gnw_h[m]), dma=True)
        l1 = b.emit("sync", lambda h, m=m: h.dma_start(out=cols4[:, 3:4], in_=gnb_h[m]), dma=True)
        t1 = b.emit("scalar", lambda h: h.activation(out=wcol[:], in_=cols4[:, 0:1], func=AF.Exp),
                    waits=[l1, prev_vec_guard])
        t2 = b.emit("scalar", lambda h: h.activation(out=wcol[:], in_=wcol[:], func=AF.Exp,
                    scale=-1.0))
        t3 = b.emit("scalar", lambda h: h.activation(out=eucol[:], in_=cols4[:, 1:2], func=AF.Exp))
        lek = b.emit("sync", lambda h, m=m: h.dma_start(out=B0[:], in_=ek_h[m * 128:(m + 1) * 128, :]),
                     dma=True, waits=[prev])
        lekv = b.emit("sync", lambda h, m=m: h.dma_start(out=B1[:], in_=ekv_h[m * 128:(m + 1) * 128, :]),
                      dma=True)
        tw1 = b.emit("vector", lambda h: h.memset(B4[:], 1.0), waits=[prev])
        tw2 = b.emit("vector", lambda h: h.tensor_scalar(out=B4[:], in0=B4[:], scalar1=wcol[:],
                     scalar2=None, op0=ALU.mult), waits=[t2])
        s1 = b.emit("vector", lambda h: h.tensor_tensor_scan(out=B2[:], data0=B4[:], data1=B1[:],
                    initial=0.0, op0=ALU.mult, op1=ALU.add), waits=[lekv, tw2])
        s2 = b.emit("vector", lambda h: h.tensor_tensor_scan(out=B3[:], data0=B4[:], data1=B0[:],
                    initial=0.0, op0=ALU.mult, op1=ALU.add), waits=[lek])
        if DEBUG_L0B:
            d1 = b.emit("gpsimd", lambda h, m=m: h.dma_start(out=dbg[1][m * 128:(m + 1) * 128, :], in_=B2[:]), dma=True, waits=[s1])
            d2 = b.emit("gpsimd", lambda h, m=m: h.dma_start(out=dbg[2][m * 128:(m + 1) * 128, :], in_=B3[:]), dma=True, waits=[s2])
        y1 = b.emit("vector", lambda h: h.scalar_tensor_tensor(out=B5[:, 1:T], in0=B1[:, 1:T],
                    scalar=eucol[:], in1=B2[:, 0:T - 1], op0=ALU.mult, op1=ALU.add),
                    waits=[s1, t3])
        y2 = b.emit("vector", lambda h: h.tensor_scalar(out=B5[:, 0:1], in0=B1[:, 0:1],
                    scalar1=eucol[:], scalar2=None, op0=ALU.mult))
        z1 = b.emit("vector", lambda h: h.scalar_tensor_tensor(out=B4[:, 1:T], in0=B0[:, 1:T],
                    scalar=eucol[:], in1=B3[:, 0:T - 1], op0=ALU.mult, op1=ALU.add), waits=[s2])
        z2 = b.emit("vector", lambda h: h.tensor_scalar(out=B4[:, 0:1], in0=B0[:, 0:1],
                    scalar1=eucol[:], scalar2=None, op0=ALU.mult))
        z3 = b.emit("vector", lambda h: h.tensor_scalar(out=B4[:], in0=B4[:], scalar1=1e-9,
                    scalar2=None, op0=ALU.add))
        if DEBUG_L0B:
            d3 = b.emit("gpsimd", lambda h, m=m: h.dma_start(out=dbg[3][m * 128:(m + 1) * 128, :], in_=B5[:]), dma=True, waits=[y2])
            d4 = b.emit("gpsimd", lambda h, m=m: h.dma_start(out=dbg[4][m * 128:(m + 1) * 128, :], in_=B4[:]), dma=True, waits=[z3])
            r1 = b.emit("vector", lambda h: h.reciprocal(out=B2[:], in_=B4[:]), waits=[d1, d4])
        else:
            r1 = b.emit("vector", lambda h: h.reciprocal(out=B2[:], in_=B4[:]))
        w1 = b.emit("vector", lambda h: h.tensor_tensor(out=B3[:], in0=B5[:], in1=B2[:],
                    op=ALU.mult), waits=[(d2 if DEBUG_L0B else None), (d3 if DEBUG_L0B else None)])
        lsr = b.emit("sync", lambda h, m=m: h.dma_start(out=B4[:], in_=sr_h[m * 128:(m + 1) * 128, :]),
                     dma=True, waits=[r1, (d4 if DEBUG_L0B else None)])
        if DEBUG_L0B:
            d5 = b.emit("gpsimd", lambda h, m=m: h.dma_start(out=dbg[5][m * 128:(m + 1) * 128, :], in_=B3[:]), dma=True, waits=[w1])
        o1 = b.emit("vector", lambda h: h.tensor_tensor(out=B5[:], in0=B3[:], in1=B4[:],
                    op=ALU.mult), waits=[lsr])
        prev_vec_guard = o1
        if DEBUG_L0B in (2, 3):
            o1 = b.emit("sync", lambda h, m=m: h.dma_start(out=B5[:],
                        in_=tn("omref", [D, T], F32, "in")[m * 128:(m + 1) * 128, :]),
                        dma=True, waits=[o1])
        sq1 = b.emit("scalar", lambda h: h.activation(out=B3[:], in_=B5[:], func=AF.Square),
                     waits=[o1])
        bs, bsf = banks.get()
        bqn, bqf = banks.get()
        ts_ = mm_chain(b, bs[0:2, 0:CW],
                       [(bo2[:], B5[:, n * CW:(n + 1) * CW]) for n in range(NTOK)],
                       waits=[o1, t_bo2], free_tok=bsf)
        tq_ = mm_chain(b, bqn[0:2, 0:CW],
                       [(bo2[:], B3[:, n * CW:(n + 1) * CW]) for n in range(NTOK)],
                       waits=[sq1], free_tok=bqf)
        if DEBUG_L0B and m == 0:
            t8 = b.emit("gpsimd", lambda h: h.dma_start(out=tn("dbg8", [128, T], F32, "Internal"), in_=B3[:]), dma=True, waits=[ts_, tq_])
            t9 = b.emit("gpsimd", lambda h: h.dma_start(out=tn("dbg9", [128, T], F32, "Internal"), in_=B5[:]), dma=True)
            t10 = b.emit("gpsimd", lambda h: h.dma_start(out=tn("dbg10", [128, 2], F32, "Internal"), in_=bo2[:]), dma=True)
            dbs = b.sb([2, 2 * CW], F32, stack=stB)
            bc1 = b.emit("vector", lambda h, bs=bs: h.tensor_copy(out=dbs[:, 0:CW], in_=bs[0:2, 0:CW]), waits=[ts_, tq_])
            bc2 = b.emit("vector", lambda h, bqn=bqn: h.tensor_copy(out=dbs[:, CW:2 * CW], in_=bqn[0:2, 0:CW]))
            b.emit("gpsimd", lambda h: h.dma_start(out=tn("dbg6", [2, 2 * CW], F32, "Internal"), in_=dbs[:]), dma=True, waits=[bc2])
            dbs2 = b.sb([128, CW], F32, stack=stB)
            bc3 = b.emit("vector", lambda h: h.tensor_copy(out=dbs2[:], in_=B5[:, 0:CW]))
            dbg_gate = b.emit("gpsimd", lambda h: h.dma_start(out=tn("dbg7", [128, CW], F32, "Internal"), in_=dbs2[:]), dma=True, waits=[bc3])
        rs1 = b.emit("vector", lambda h, bs=bs: h.tensor_reduce(out=rows2[:, 0:1],
                     in_=bs[0:2, 0:CW], axis=AX.X, op=ALU.add), waits=[ts_, tq_])
        rs2 = b.emit("vector", lambda h, bqn=bqn: h.tensor_reduce(out=rows2[:, 1:2],
                     in_=bqn[0:2, 0:CW], axis=AX.X, op=ALU.add))
        c1 = b.emit("vector", lambda h: h.tensor_scalar(out=rows2[:, 2:3], in0=rows2[:, 0:1],
                    scalar1=1.0 / (T * HD), scalar2=None, op0=ALU.mult))
        c2 = b.emit("vector", lambda h: h.tensor_scalar(out=rows2[:, 3:4], in0=rows2[:, 1:2],
                    scalar1=1.0 / (T * HD), scalar2=None, op0=ALU.mult))
        c3 = b.emit("vector", lambda h: h.tensor_tensor(out=rows2[:, 4:5], in0=rows2[:, 2:3],
                    in1=rows2[:, 2:3], op=ALU.mult))
        c4 = b.emit("vector", lambda h: h.tensor_tensor(out=rows2[:, 4:5], in0=rows2[:, 3:4],
                    in1=rows2[:, 4:5], op=ALU.subtract))
        c5 = b.emit("scalar", lambda h: h.activation(out=rows2[:, 4:5], in_=rows2[:, 4:5],
                    func=AF.Sqrt, bias=epsB[0:2, 0:1]),
                    waits=[c4] + ([dbg_gate, t8, t9, t10] if (DEBUG_L0B and m == 0) else []))
        c6 = b.emit("vector", lambda h: h.reciprocal(out=rows2[:, 5:6], in_=rows2[:, 4:5]),
                    waits=[c5])
        banks.set_free(bs, c6); banks.set_free(bqn, c6)
        b.emit("gpsimd", lambda h, m=m: h.dma_start(
            out=tn("dbgrows", [16, 6], F32, "Internal")[2 * m:2 * m + 2, :], in_=rows2[:]),
            dma=True, waits=[c6])
        m1 = b.emit("vector", lambda h: h.tensor_copy(out=mv[:, 0:1], in_=rows2[:, 2:3]))
        m2 = b.emit("vector", lambda h: h.tensor_copy(out=mv[:, 1:2], in_=rows2[:, 5:6]))
        brd, brdf = banks.get()
        tb = mm_chain(b, brd[:, 0:2], [(crep2[:], mv[:])], waits=[m2, t_crep2], free_tok=brdf)
        e1 = b.emit("vector", lambda h, brd=brd: h.tensor_copy(out=murs[:], in_=brd[:, 0:2]),
                    waits=[tb])
        banks.set_free(brd, e1)
        a1 = b.emit("vector", lambda h: h.tensor_tensor(out=acol[:], in0=murs[:, 1:2],
                    in1=cols4[:, 2:3], op=ALU.mult), waits=[e1])
        b1_ = b.emit("vector", lambda h: h.tensor_scalar(out=bcol[:], in0=murs[:, 0:1],
                     scalar1=acol[:], scalar2=-1.0, op0=ALU.mult, op1=ALU.mult))
        b2_ = b.emit("vector", lambda h: h.tensor_tensor(out=bcol[:], in0=bcol[:],
                     in1=cols4[:, 3:4], op=ALU.add))
        g1 = b.emit("vector", lambda h: h.tensor_scalar(out=ybt[:], in0=B5[:], scalar1=acol[:],
                    scalar2=bcol[:], op0=ALU.mult, op1=ALU.add))
        prev = b.emit("gpsimd", lambda h, m=m: h.dma_start(out=yb_h[m * 128:(m + 1) * 128, :],
                      in_=ybt[:]), dma=True, waits=[g1])
    stB.close()
    b.barrier()
    banks.reset()

    stD = contextlib.ExitStack()
    wo, two = cast_resident2(b, stD, tn("l0wo", [D, D], F32, "in"))
    ones_col = b.sb([128, 128], F32, stack=stD)
    t_ones = b.emit("vector", lambda h: h.memset(ones_col[:], 1.0))
    epsD = b.sb([128, 1], F32, stack=stD)
    b.emit("vector", lambda h: h.memset(epsD[:], 1e-5))
    ybs = [b.sb([128, CW], BF16, stack=stD) for _ in range(NK)]
    xsD = [b.sb([128, CW], F32, stack=stD) for _ in range(NK)]
    res = [b.sb([128, CW], F32, stack=stD) for _ in range(NK)]
    sqt = b.sb([128, CW], F32, stack=stD)
    rows = b.sb([1, 4 * CW], F32, stack=stD)
    buf_free = None
    for n in range(NTOK):
        lts = []
        for k in range(NK):
            l1 = b.emit("sync", lambda h, k=k, n=n: h.dma_start(out=ybs[k][:],
                        in_=yb_h[k * 128:(k + 1) * 128, n * CW:(n + 1) * CW]), dma=True,
                        waits=[buf_free])
            l2 = b.emit("sync", lambda h, k=k, n=n: h.dma_start(out=xsD[k][:],
                        in_=xT[k * 128:(k + 1) * 128, n * CW:(n + 1) * CW]), dma=True)
            lts.append(l2)
        ev = None
        for m in range(NK):
            bank, bfree = banks.get()
            mmt = mm_chain(b, bank[:, 0:CW], [(wo[k][:, m * 128:(m + 1) * 128], ybs[k][:])
                                               for k in range(NK)],
                           waits=[lts[-1], two], free_tok=bfree)
            ev = b.emit("vector", lambda h, m=m, bank=bank: h.tensor_tensor(out=res[m][:],
                        in0=bank[:, 0:CW], in1=xsD[m][:], op=ALU.add), waits=[mmt])
            banks.set_free(bank, ev)
        vt = ln_chunk(b, stD, banks, [r[:] for r in res], CW, ones_col, epsD,
                      [r[:] for r in res], waits=[ev, t_ones], sq=sqt, rows=rows)
        sts = []
        for m in range(NK):
            sts.append(b.emit("gpsimd", lambda h, m=m, n=n: h.dma_start(
                out=x1T[m * 128:(m + 1) * 128, n * CW:(n + 1) * CW], in_=res[m][:]),
                dma=True, waits=[vt]))
        buf_free = sts[-1]
    stD.close()
    b.barrier()
    banks.reset()
    return b.tok("gpsimd", dma=True)


def _mk_launch(stage_fns, boundary_kinds):
    nc = bass.Bass("TRN2", target_bir_lowering=False, detect_race_conditions=False)
    stack = contextlib.ExitStack()
    b = Builder(nc, stack)
    tensors = {}

    def tn(name, shape, dt, kind):
        if name in tensors:
            return tensors[name]
        k = boundary_kinds.get(name, kind)
        if k == "in":
            bk = "ExternalInput"
        elif k.startswith("out"):
            bk = "ExternalOutput"
        else:
            bk = "Internal"
        ap = nc.dram_tensor(name, shape, dt, kind=bk).ap()
        tensors[name] = ap
        return ap

    psum = [b.ps([128, 512]) for _ in range(8)]
    banks = Banks(b, psum)
    fin = []
    for fn in stage_fns:
        fin.append(fn(b, tn, banks))
    b.finalize(fin)
    return nc, stack


def _consts():
    c = {}
    c["c_rep2"] = np.zeros((2, 128), np.float32)
    c["c_rep2"][0, :64] = 1.0
    c["c_rep2"][1, 64:] = 1.0
    c["cid"] = np.eye(128, dtype=np.float32)
    ch = np.zeros((8, 16, 128), np.float32)
    for t in range(8):
        ch[t, 2 * t, :64] = 1.0
        ch[t, 2 * t + 1, 64:] = 1.0
    c["chead"] = ch
    cs16 = np.zeros((16, 16 * 128), np.float32)
    for hh in range(16):
        cs16[hh, hh * 128:(hh + 1) * 128] = 1.0
    c["csel16"] = cs16
    cs8 = np.zeros((8, 8 * 128), np.float32)
    for e in range(8):
        cs8[e, e * 128:(e + 1) * 128] = 1.0
    c["csel8"] = cs8
    c["celast"] = np.zeros((128, 1), np.float32)
    c["celast"][127, 0] = 1.0
    one = np.ones((128, 128), np.float32)
    c["mUS"] = np.triu(one, 1).copy()
    c["mUI"] = np.triu(one, 0).copy()
    c["mLS"] = np.tril(one, -1).copy()
    return c


def _prep_l0_attn(inputs):
    f = np.float32
    d = {}
    d["l0wr"] = np.ascontiguousarray(np.asarray(inputs["L0_attn_W_r"], f).T)
    d["l0wk"] = np.ascontiguousarray(np.asarray(inputs["L0_attn_W_k"], f).T)
    d["l0wv"] = np.ascontiguousarray(np.asarray(inputs["L0_attn_W_v"], f).T)
    d["l0wo"] = np.ascontiguousarray(np.asarray(inputs["L0_attn_W_o"], f).T)
    for nm, key in (("l0tmr", "time_mix_r"), ("l0tmk", "time_mix_k"), ("l0tmv", "time_mix_v"),
                    ("l0gnw", "gn_weight"), ("l0gnb", "gn_bias")):
        d[nm] = np.ascontiguousarray(np.asarray(inputs["L0_attn_" + key], f).reshape(8, 128, 1))
    d["l0dec"] = np.ascontiguousarray(np.asarray(inputs["L0_attn_time_decay"], f).reshape(8, 128, 1))
    d["l0fir"] = np.ascontiguousarray(np.asarray(inputs["L0_attn_time_first"], f).reshape(8, 128, 1))
    d["c_rep2"] = _consts()["c_rep2"]
    return d


def stage_l0_ffn(b, tn, banks):
    st = contextlib.ExitStack()
    x1T = tn("x1T", [D, T], F32, "in")
    x2T = tn("x2T", [D, T], F32, "out_x2")
    x1b_h = tn("x1b_h", [D, T], BF16, "Internal")
    w1b_h = tn("w1b_h", [8, 32, 128, 1024], BF16, "Internal")
    w2b_h = tn("w2b_h", [8, 8, 128, 4096], BF16, "Internal")
    cid_h = tn("cid", [128, 128], F32, "in")
    sel8_h = tn("csel8", [8, 1024], F32, "in")
    rwT_h = tn("rwT", [D, 8], F32, "in")
    rb_h = tn("rb", [8, 1], F32, "in")

    logits = b.sb([8, T], F32, stack=st)
    cid = b.sb([128, 128], F32, stack=st)
    cw_cm = b.sb([8, T], F32, stack=st)
    t_cid = b.emit("sync", lambda h: h.dma_start(out=cid[:], in_=cid_h), dma=True)

    st1 = contextlib.ExitStack()
    xf = [b.sb([128, CW], F32, stack=st1) for _ in range(NK)]
    xbtile = b.sb([128, CW], BF16, stack=st1)
    rw = []
    t_rw = None
    for k in range(NK):
        c, tc = load_col(b, st1, rwT_h[k * 128:(k + 1) * 128, :], w=8)
        rw.append(c); t_rw = tc
    rbc = b.sb([128, 1], F32, stack=st1)
    t_rb = b.emit("sync", lambda h: h.dma_start(out=rbc[0:8, 0:1], in_=rb_h), dma=True)
    rbc8 = rbc[0:8, 0:1]
    buf_free = None
    for n in range(NTOK):
        lts = []
        for k in range(NK):
            lts.append(b.emit("sync", lambda h, k=k, n=n: h.dma_start(out=xf[k][:],
                       in_=x1T[k * 128:(k + 1) * 128, n * CW:(n + 1) * CW]), dma=True,
                       waits=[buf_free]))
        cts = []
        prev_st = None
        for k in range(NK):
            ct = b.emit("vector", lambda h, k=k: h.tensor_copy(out=xbtile[:], in_=xf[k][:]),
                        waits=[lts[k], buf_free, prev_st])
            prev_st = b.emit("gpsimd", lambda h, k=k, n=n: h.dma_start(
                out=x1b_h[k * 128:(k + 1) * 128, n * CW:(n + 1) * CW], in_=xbtile[:]),
                dma=True, waits=[ct])
            cts.append(prev_st)
        bank, bfree = banks.get()
        mmt = mm_chain(b, bank[0:8, 0:CW], [(rw[k][:], xf[k][:]) for k in range(NK)],
                       waits=[t_rw, lts[-1]], free_tok=bfree)
        ev = b.emit("vector", lambda h, bank=bank, n=n: h.tensor_scalar(
            out=logits[:, n * CW:(n + 1) * CW], in0=bank[0:8, 0:CW], scalar1=rbc8,
            scalar2=None, op0=ALU.add), waits=[mmt, t_rb])
        banks.set_free(bank, ev)
        buf_free = [ev, cts[-1]]
    t_logits = b.tok("vector")
    st1.close()
    b.barrier(); banks.reset()

    st2 = contextlib.ExitStack()
    ltok = b.sb([128, 8], F32, stack=st2)
    trow = b.sb([128, 8], F32, stack=st2)
    expt = b.sb([128, 8], F32, stack=st2)
    gem = b.sb([128, 8], F32, stack=st2)
    colb = b.sb([128, 4], F32, stack=st2)
    for i in range(32):
        bt, btf = banks.get()
        t1 = mm_chain(b, bt[:, 0:8], [(logits[:, i * 128:(i + 1) * 128], cid[0:8, 0:8])],
                      waits=[t_logits, t_cid], free_tok=btf)
        c1 = b.emit("vector", lambda h, bt=bt: h.tensor_copy(out=ltok[:], in_=bt[:, 0:8]),
                    waits=[t1])
        banks.set_free(bt, c1)
        v1 = b.emit("vector", lambda h: h.tensor_reduce(out=colb[:, 0:1], in_=ltok[:],
                    axis=AX.X, op=ALU.max))
        v2 = b.emit("vector", lambda h: h.tensor_scalar(out=trow[:], in0=ltok[:],
                    scalar1=colb[:, 0:1], scalar2=None, op0=ALU.subtract))
        v3 = b.emit("vector", lambda h: h.tensor_single_scalar(out=gem[:], in_=trow[:],
                    scalar=0.0, op=ALU.is_ge))
        v4 = b.emit("vector", lambda h: h.scalar_tensor_tensor(out=gem[:], in0=gem[:],
                    scalar=-1e30, in1=trow[:], op0=ALU.mult, op1=ALU.add))
        v5 = b.emit("vector", lambda h: h.tensor_reduce(out=colb[:, 1:2], in_=gem[:],
                    axis=AX.X, op=ALU.max))
        s1 = b.emit("scalar", lambda h: h.activation(out=expt[:], in_=trow[:], func=AF.Exp),
                    waits=[v2])
        s2 = b.emit("scalar", lambda h: h.activation(out=colb[:, 2:3], in_=colb[:, 1:2],
                    func=AF.Exp), waits=[v5])
        v6 = b.emit("vector", lambda h: h.tensor_scalar(out=colb[:, 2:3], in0=colb[:, 2:3],
                    scalar1=1.0, scalar2=None, op0=ALU.add), waits=[s2])
        v7 = b.emit("vector", lambda h: h.reciprocal(out=colb[:, 3:4], in_=colb[:, 2:3]))
        v8 = b.emit("vector", lambda h: h.tensor_scalar(out=gem[:], in0=trow[:],
                    scalar1=colb[:, 1:2], scalar2=None, op0=ALU.is_ge))
        v9 = b.emit("vector", lambda h: h.tensor_tensor(out=expt[:], in0=gem[:], in1=expt[:],
                    op=ALU.mult), waits=[s1])
        v10 = b.emit("vector", lambda h: h.tensor_scalar(out=expt[:], in0=expt[:],
                     scalar1=colb[:, 3:4], scalar2=None, op0=ALU.mult))
        bt2, btf2 = banks.get()
        t2_ = mm_chain(b, bt2[0:8, 0:128], [(expt[:], cid[:])], waits=[v10], free_tok=btf2)
        e2_ = b.emit("vector", lambda h, bt2=bt2, i=i: h.tensor_copy(
            out=cw_cm[:, i * 128:(i + 1) * 128], in_=bt2[0:8, 0:128]), waits=[t2_])
        banks.set_free(bt2, e2_)
    t_cw = b.tok("vector")
    b.emit("gpsimd", lambda h: h.dma_start(out=tn("cwdbg", [8, T], F32, "Internal"),
           in_=cw_cm[:]), dma=True, waits=[t_cw])
    st2.close()
    b.barrier(); banks.reset()

    st3 = contextlib.ExitStack()
    w1r_h = tn("w1r", [8, 32, 128, 1024], F32, "in")
    w2r_h = tn("w2r", [8, 8, 128, 4096], F32, "in")
    stg1 = [b.sb([128, 4096], F32, stack=st3) for _ in range(2)]
    stgb = [b.sb([128, 4096], BF16, stack=st3) for _ in range(2)]
    slot_free = [None, None]
    si = 0
    for e in range(8):
        for m in range(32):
            s_ = si % 2; si += 1
            lt = b.emit("sync", lambda h, e=e, m=m, s_=s_: h.dma_start(
                out=stg1[s_][:, 0:1024], in_=w1r_h[e, m]), dma=True, waits=[slot_free[s_]])
            ct = b.emit("vector", lambda h, s_=s_: h.tensor_copy(out=stgb[s_][:, 0:1024],
                        in_=stg1[s_][:, 0:1024]), waits=[lt])
            st_ = b.emit("gpsimd", lambda h, e=e, m=m, s_=s_: h.dma_start(
                out=w1b_h[e, m], in_=stgb[s_][:, 0:1024]), dma=True, waits=[ct])
            slot_free[s_] = st_
        for m2 in range(8):
            s_ = si % 2; si += 1
            lt = b.emit("sync", lambda h, e=e, m2=m2, s_=s_: h.dma_start(
                out=stg1[s_][:], in_=w2r_h[e, m2]), dma=True, waits=[slot_free[s_]])
            ct = b.emit("vector", lambda h, s_=s_: h.tensor_copy(out=stgb[s_][:],
                        in_=stg1[s_][:]), waits=[lt])
            st_ = b.emit("gpsimd", lambda h, e=e, m2=m2, s_=s_: h.dma_start(
                out=w2b_h[e, m2], in_=stgb[s_][:]), dma=True, waits=[ct])
            slot_free[s_] = st_
    st3.close()
    b.barrier(); banks.reset()

    st4 = contextlib.ExitStack()
    QW = 512
    sel8 = b.sb([8, 1024], F32, stack=st4)
    t_sel = b.emit("sync", lambda h: h.dma_start(out=sel8[:], in_=sel8_h), dma=True)
    x1bq = [b.sb([128, QW], BF16, stack=st4) for _ in range(NK)]
    y_acc = [b.sb([128, QW], F32, stack=st4) for _ in range(NK)]
    h1bf = [b.sb([128, QW], BF16, stack=st4) for _ in range(32)]
    w1s = [b.sb([128, 1024], BF16, stack=st4) for _ in range(2)]
    w2s = [b.sb([128, 4096], BF16, stack=st4) for _ in range(2)]
    b1s = b.sb([128, 32], F32, stack=st4)
    b2s = b.sb([128, 8], F32, stack=st4)
    cwrep = b.sb([128, QW], F32, stack=st4)
    xfq = [b.sb([128, CW], F32, stack=st4) for _ in range(NK)]
    sqt = b.sb([128, CW], F32, stack=st4)
    rows = b.sb([1, 4 * CW], F32, stack=st4)
    vtile = [b.sb([128, CW], F32, stack=st4) for _ in range(NK)]
    ones_col = b.sb([128, 128], F32, stack=st4)
    t_ones = b.emit("vector", lambda h: h.memset(ones_col[:], 1.0))
    epsc = b.sb([128, 2], F32, stack=st4)
    b.emit("vector", lambda h: h.memset(epsc[:, 0:1], 1e-12))
    b.emit("vector", lambda h: h.memset(epsc[:, 1:2], 1e-5))
    lnw_h = tn("lnw", [NK, 128, 1], F32, "in")
    lnb_h = tn("lnb", [NK, 128, 1], F32, "in")
    b1r_h = tn("b1r", [8, 128, 32], F32, "in")
    b2r_h = tn("b2r", [8, 128, 8], F32, "in")
    lnwc = []; lnbc = []
    for k in range(NK):
        c1_, t1_ = load_col(b, st4, lnw_h[k]); c2_, t2_ = load_col(b, st4, lnb_h[k])
        lnwc.append(c1_); lnbc.append(c2_)
    t_lncols = t2_
    q_free = None
    for q in range(T // QW):
        lq = []
        for k in range(NK):
            lq.append(b.emit("sync", lambda h, k=k, q=q: h.dma_start(out=x1bq[k][:],
                      in_=x1b_h[k * 128:(k + 1) * 128, q * QW:(q + 1) * QW]), dma=True,
                      waits=[q_free]))
        mz = None
        for m2 in range(NK):
            mz = b.emit("vector", lambda h, m2=m2: h.memset(y_acc[m2][:], 0.0), waits=[q_free])
        w1_free = [None, None]; w2_free = [None, None]
        h1_done = None
        for e in range(8):
            lb1 = b.emit("sync", lambda h, e=e: h.dma_start(out=b1s[:], in_=b1r_h[e]),
                         dma=True, waits=[h1_done])
            lb2 = b.emit("sync", lambda h, e=e: h.dma_start(out=b2s[:], in_=b2r_h[e]), dma=True)
            cwr_toks = []
            for n2 in range(QW // CW):
                bk, bkf = banks.get()
                tt = mm_chain(b, bk[:, 0:CW],
                              [(sel8[:, e * 128:(e + 1) * 128],
                                cw_cm[:, q * QW + n2 * CW:q * QW + (n2 + 1) * CW])],
                              waits=[t_sel, t_cw], free_tok=bkf)
                ce = b.emit("vector", lambda h, bk=bk, n2=n2: h.tensor_copy(
                    out=cwrep[:, n2 * CW:(n2 + 1) * CW], in_=bk[:, 0:CW]), waits=[tt])
                banks.set_free(bk, ce)
                cwr_toks.append(ce)
            gl_tok = None
            for m in range(32):
                s_ = m % 2
                lw = b.emit("sync", lambda h, e=e, m=m, s_=s_: h.dma_start(out=w1s[s_][:],
                            in_=w1b_h[e, m]), dma=True, waits=[w1_free[s_]])
                for n2 in range(QW // CW):
                    bk, bkf = banks.get()
                    mmt = mm_chain(b, bk[:, 0:CW],
                                   [(w1s[s_][:, k * 128:(k + 1) * 128],
                                     x1bq[k][:, n2 * CW:(n2 + 1) * CW]) for k in range(NK)],
                                   waits=[lw, lq[-1]], free_tok=bkf)
                    gl_tok = b.emit("scalar", lambda h, bk=bk, m=m, n2=n2: h.activation(
                        out=h1bf[m][:, n2 * CW:(n2 + 1) * CW], in_=bk[:, 0:CW], func=AF.Gelu,
                        bias=b1s[:, m:m + 1]), waits=[mmt, lb1])
                    banks.set_free(bk, gl_tok)
                w1_free[s_] = gl_tok
            y_tok = None
            for m2 in range(NK):
                s_ = m2 % 2
                lw2 = b.emit("sync", lambda h, e=e, m2=m2, s_=s_: h.dma_start(out=w2s[s_][:],
                             in_=w2b_h[e, m2]), dma=True, waits=[w2_free[s_]])
                for n2 in range(QW // CW):
                    bk, bkf = banks.get()
                    mmt = mm_chain(b, bk[:, 0:CW],
                                   [(w2s[s_][:, k2 * 128:(k2 + 1) * 128],
                                     h1bf[k2][:, n2 * CW:(n2 + 1) * CW]) for k2 in range(32)],
                                   waits=[lw2, gl_tok], free_tok=bkf)
                    tv1 = b.emit("vector", lambda h, bk=bk, m2=m2, n2=n2: h.scalar_tensor_tensor(
                        out=sqt[:], in0=bk[:, 0:CW], scalar=b2s[:, m2:m2 + 1],
                        in1=cwrep[:, n2 * CW:(n2 + 1) * CW], op0=ALU.add, op1=ALU.mult),
                        waits=[mmt, lb2, cwr_toks[n2]])
                    y_tok = b.emit("vector", lambda h, m2=m2, n2=n2: h.tensor_tensor(
                        out=y_acc[m2][:, n2 * CW:(n2 + 1) * CW],
                        in0=y_acc[m2][:, n2 * CW:(n2 + 1) * CW], in1=sqt[:], op=ALU.add),
                        waits=[mz])
                    banks.set_free(bk, tv1)
                w2_free[s_] = y_tok
            h1_done = y_tok
        dmoe = None
        for m2 in range(NK):
            dmoe = b.emit("gpsimd", lambda h, m2=m2, q=q: h.dma_start(
                out=tn("moedbg", [D, T], F32, "Internal")[m2 * 128:(m2 + 1) * 128, q * QW:(q + 1) * QW],
                in_=y_acc[m2][:]), dma=True, waits=[h1_done])
        ln_toks = []
        for n2 in range(QW // CW):
            lxf = []
            for k in range(NK):
                lxf.append(b.emit("sync", lambda h, k=k, q=q, n2=n2: h.dma_start(
                    out=xfq[k][:], in_=x1T[k * 128:(k + 1) * 128,
                                          q * QW + n2 * CW:q * QW + (n2 + 1) * CW]),
                    dma=True, waits=[q_free, ln_toks[-1] if ln_toks else None]))
            ut = None
            for m2 in range(NK):
                ut = b.emit("vector", lambda h, m2=m2, n2=n2: h.tensor_tensor(
                    out=vtile[m2][:], in0=y_acc[m2][:, n2 * CW:(n2 + 1) * CW],
                    in1=xfq[m2][:], op=ALU.add), waits=[lxf[m2], h1_done])
            vt = ln_chunk(b, st4, banks, [v_[:] for v_ in vtile], CW, ones_col,
                          epsc[:, 0:1], [v_[:] for v_ in vtile], affine=(lnwc, lnbc),
                          waits=[ut, t_ones, t_lncols], sq=sqt, rows=rows)
            ut2 = None
            for m2 in range(NK):
                ut2 = b.emit("vector", lambda h, m2=m2: h.tensor_tensor(
                    out=vtile[m2][:], in0=vtile[m2][:], in1=xfq[m2][:], op=ALU.add),
                    waits=[vt])
            vt2 = ln_chunk(b, st4, banks, [v_[:] for v_ in vtile], CW, ones_col,
                           epsc[:, 1:2], [v_[:] for v_ in vtile], affine=None,
                           waits=[ut2, t_ones], sq=sqt, rows=rows)
            sts = []
            for m2 in range(NK):
                sts.append(b.emit("gpsimd", lambda h, m2=m2, q=q, n2=n2: h.dma_start(
                    out=x2T[m2 * 128:(m2 + 1) * 128, q * QW + n2 * CW:q * QW + (n2 + 1) * CW],
                    in_=vtile[m2][:]), dma=True, waits=[vt2]))
            ln_toks.append(sts[-1])
        q_free = [ln_toks[-1], b.tok("tensor"), dmoe]
    st4.close()
    b.barrier(); banks.reset()
    return b.tok("gpsimd", dma=True)


def _prep_l0_ffn(inputs):
    f = np.float32
    c = _consts()
    d = {}
    d["rwT"] = np.ascontiguousarray(np.asarray(inputs["L0_ffn_router_w"], f).T)
    d["rb"] = np.asarray(inputs["L0_ffn_router_b"], f).reshape(8, 1).copy()
    w1 = np.asarray(inputs["L0_ffn_w1"], f)
    d["w1r"] = np.ascontiguousarray(w1.reshape(8, 32, 128, 8, 128).transpose(0, 1, 4, 3, 2)
                                    .reshape(8, 32, 128, 1024))
    w2 = np.asarray(inputs["L0_ffn_w2"], f)
    d["w2r"] = np.ascontiguousarray(w2.reshape(8, 8, 128, 32, 128).transpose(0, 1, 4, 3, 2)
                                    .reshape(8, 8, 128, 4096))
    d["b1r"] = np.ascontiguousarray(np.asarray(inputs["L0_ffn_b1"], f).reshape(8, 32, 128)
                                    .transpose(0, 2, 1))
    d["b2r"] = np.ascontiguousarray(np.asarray(inputs["L0_ffn_b2"], f).reshape(8, 8, 128)
                                    .transpose(0, 2, 1))
    d["lnw"] = np.ascontiguousarray(np.asarray(inputs["L0_ffn_ln_w"], f).reshape(8, 128, 1))
    d["lnb"] = np.ascontiguousarray(np.asarray(inputs["L0_ffn_ln_b"], f).reshape(8, 128, 1))
    d["cid"] = c["cid"]
    d["csel8"] = c["csel8"]
    return d


def stage_l1_ffn(b, tn, banks):
    st = contextlib.ExitStack()
    x3T = tn("x3T", [D, T], F32, "in")
    outT = tn("outT", [D, T], F32, "out_final")
    x3b_h = tn("x3b_h", [D, T], BF16, "Internal")
    wgb_h = tn("wgb_h", [16, 8, 128, 1024], BF16, "Internal")
    wub_h = tn("wub_h", [16, 8, 128, 1024], BF16, "Internal")
    wdb_h = tn("wdb_h", [16, 8, 128, 1024], BF16, "Internal")
    sgb_h = tn("sgb_h", [8, 128, 1024], BF16, "Internal")
    sub_h = tn("sub_h", [8, 128, 1024], BF16, "Internal")
    sdb_h = tn("sdb_h", [8, 128, 1024], BF16, "Internal")
    cid_h = tn("cid", [128, 128], F32, "in")
    sel16_h = tn("csel16", [16, 2048], F32, "in")
    gwT_h = tn("gwT", [D, 16], F32, "in")
    eb_h = tn("ebias", [1, 16], F32, "in")

    scores = b.sb([16, T], F32, stack=st)
    comb_cm = b.sb([16, T], F32, stack=st)
    cid = b.sb([128, 128], F32, stack=st)
    t_cid = b.emit("sync", lambda h: h.dma_start(out=cid[:], in_=cid_h), dma=True)

    st1 = contextlib.ExitStack()
    xf = [b.sb([128, CW], F32, stack=st1) for _ in range(NK)]
    xbtile = b.sb([128, CW], BF16, stack=st1)
    gw = []
    t_gw = None
    for k in range(NK):
        c, t_gw = load_col(b, st1, gwT_h[k * 128:(k + 1) * 128, :], w=16)
        gw.append(c)
    buf_free = None
    for n in range(NTOK):
        lts = []
        for k in range(NK):
            lts.append(b.emit("sync", lambda h, k=k, n=n: h.dma_start(out=xf[k][:],
                       in_=x3T[k * 128:(k + 1) * 128, n * CW:(n + 1) * CW]), dma=True,
                       waits=[buf_free]))
        prev_st = None
        for k in range(NK):
            ct = b.emit("vector", lambda h, k=k: h.tensor_copy(out=xbtile[:], in_=xf[k][:]),
                        waits=[lts[k], buf_free, prev_st])
            prev_st = b.emit("gpsimd", lambda h, k=k, n=n: h.dma_start(
                out=x3b_h[k * 128:(k + 1) * 128, n * CW:(n + 1) * CW], in_=xbtile[:]),
                dma=True, waits=[ct])
        bank, bfree = banks.get()
        mmt = mm_chain(b, bank[0:16, 0:CW], [(gw[k][:], xf[k][:]) for k in range(NK)],
                       waits=[t_gw, lts[-1]], free_tok=bfree)
        ev = b.emit("scalar", lambda h, bank=bank, n=n: h.activation(
            out=scores[:, n * CW:(n + 1) * CW], in_=bank[0:16, 0:CW], func=AF.Sigmoid),
            waits=[mmt])
        banks.set_free(bank, ev)
        buf_free = [ev, prev_st]
    t_sc = b.tok("scalar")
    st1.close()
    b.barrier(); banks.reset()

    st2 = contextlib.ExitStack()
    ebr = b.sb([1, 16], F32, stack=st2)
    t_eb = b.emit("sync", lambda h: h.dma_start(out=ebr[:], in_=eb_h), dma=True)
    ones_row = b.sb([1, 128], F32, stack=st2)
    t_or = b.emit("vector", lambda h: h.memset(ones_row[:], 1.0))
    brep = b.sb([128, 16], F32, stack=st2)
    bk0, bk0f = banks.get()
    t_br = mm_chain(b, bk0[:, 0:16], [(ones_row[:], ebr[:])], waits=[t_eb, t_or], free_tok=bk0f)
    t_bre = b.emit("vector", lambda h: h.tensor_copy(out=brep[:], in_=bk0[:, 0:16]), waits=[t_br])
    banks.set_free(bk0, t_bre)
    stok = b.sb([128, 16], F32, stack=st2)
    sel = b.sb([128, 16], F32, stack=st2)
    wrk = b.sb([128, 16], F32, stack=st2)
    cmb = b.sb([128, 16], F32, stack=st2)
    colc = b.sb([128, 4], F32, stack=st2)
    for i in range(32):
        bt, btf = banks.get()
        t1 = mm_chain(b, bt[:, 0:16], [(scores[:, i * 128:(i + 1) * 128], cid[0:16, 0:16])],
                      waits=[t_sc, t_cid], free_tok=btf)
        c1 = b.emit("vector", lambda h, bt=bt: h.tensor_copy(out=stok[:], in_=bt[:, 0:16]),
                    waits=[t1])
        banks.set_free(bt, c1)
        v1 = b.emit("vector", lambda h: h.tensor_tensor(out=sel[:], in0=stok[:], in1=brep[:],
                    op=ALU.add), waits=[t_bre])
        v2 = b.emit("vector", lambda h: h.tensor_copy(out=wrk[:], in_=sel[:]))
        vm = None
        for it in range(7):
            vm = b.emit("vector", lambda h: h.tensor_reduce(out=colc[:, 0:1], in_=wrk[:],
                        axis=AX.X, op=ALU.max))
            vm = b.emit("vector", lambda h: h.tensor_scalar(out=colc[:, 1:2], in0=colc[:, 0:1],
                        scalar1=1.0, scalar2=None, op0=ALU.mult))
            vm = b.emit("vector", lambda h: h.tensor_scalar(out=cmb[:], in0=wrk[:],
                        scalar1=colc[:, 0:1], scalar2=None, op0=ALU.is_ge))
            vm = b.emit("vector", lambda h: h.scalar_tensor_tensor(out=wrk[:], in0=cmb[:],
                        scalar=-1e30, in1=wrk[:], op0=ALU.mult, op1=ALU.add))
        v3 = b.emit("vector", lambda h: h.tensor_reduce(out=colc[:, 2:3], in_=wrk[:],
                    axis=AX.X, op=ALU.max))
        v4 = b.emit("vector", lambda h: h.tensor_scalar(out=cmb[:], in0=sel[:],
                    scalar1=colc[:, 2:3], scalar2=None, op0=ALU.is_ge))
        v5 = b.emit("vector", lambda h: h.tensor_tensor(out=cmb[:], in0=cmb[:], in1=stok[:],
                    op=ALU.mult))
        v6 = b.emit("vector", lambda h: h.tensor_reduce(out=colc[:, 3:4], in_=cmb[:],
                    axis=AX.X, op=ALU.add))
        v7 = b.emit("vector", lambda h: h.reciprocal(out=colc[:, 3:4], in_=colc[:, 3:4]))
        v8 = b.emit("vector", lambda h: h.tensor_scalar(out=cmb[:], in0=cmb[:],
                    scalar1=colc[:, 3:4], scalar2=2.826, op0=ALU.mult, op1=ALU.mult))
        bt2, btf2 = banks.get()
        t2_ = mm_chain(b, bt2[0:16, 0:128], [(cmb[:], cid[:])], waits=[v8], free_tok=btf2)
        e2_ = b.emit("vector", lambda h, bt2=bt2, i=i: h.tensor_copy(
            out=comb_cm[:, i * 128:(i + 1) * 128], in_=bt2[0:16, 0:128]), waits=[t2_])
        banks.set_free(bt2, e2_)
    t_cw = b.tok("vector")
    st2.close()
    b.barrier(); banks.reset()

    st3 = contextlib.ExitStack()
    wgr_h = tn("wgr", [16, 8, 128, 1024], F32, "in")
    wur_h = tn("wur", [16, 8, 128, 1024], F32, "in")
    wdr_h = tn("wdr", [16, 8, 128, 1024], F32, "in")
    sgr_h = tn("sgr", [8, 128, 1024], F32, "in")
    sur_h = tn("sur", [8, 128, 1024], F32, "in")
    sdr_h = tn("sdr", [8, 128, 1024], F32, "in")
    stg1 = [b.sb([128, 1024], F32, stack=st3) for _ in range(2)]
    stgb = [b.sb([128, 1024], BF16, stack=st3) for _ in range(2)]
    slot_free = [None, None]
    si = 0
    def castpass(src_ap, dst_ap):
        nonlocal si, slot_free
        s_ = si % 2; si += 1
        lt = b.emit("sync", lambda h, s_=s_, src=src_ap: h.dma_start(out=stg1[s_][:], in_=src),
                    dma=True, waits=[slot_free[s_]])
        ct = b.emit("vector", lambda h, s_=s_: h.tensor_copy(out=stgb[s_][:], in_=stg1[s_][:]),
                    waits=[lt])
        st_ = b.emit("gpsimd", lambda h, s_=s_, dst=dst_ap: h.dma_start(out=dst, in_=stgb[s_][:]),
                     dma=True, waits=[ct])
        slot_free[s_] = st_
    for e in range(16):
        for m in range(8):
            castpass(wgr_h[e, m], wgb_h[e, m])
            castpass(wur_h[e, m], wub_h[e, m])
            castpass(wdr_h[e, m], wdb_h[e, m])
    for m in range(8):
        castpass(sgr_h[m], sgb_h[m])
        castpass(sur_h[m], sub_h[m])
        castpass(sdr_h[m], sdb_h[m])
    st3.close()
    b.barrier(); banks.reset()

    st4 = contextlib.ExitStack()
    QW = 1024
    NQ = T // QW
    NC2 = QW // CW
    sel16 = b.sb([16, 2048], F32, stack=st4)
    t_sel = b.emit("sync", lambda h: h.dma_start(out=sel16[:], in_=sel16_h), dma=True)
    x3bq = [b.sb([128, QW], BF16, stack=st4) for _ in range(NK)]
    y_acc = [b.sb([128, QW], F32, stack=st4) for _ in range(NK)]
    hbf = [b.sb([128, QW], BF16, stack=st4) for _ in range(NK)]
    hgs = b.sb([128, CW], F32, stack=st4)
    wgs = [b.sb([128, 1024], BF16, stack=st4) for _ in range(2)]
    wus = [b.sb([128, 1024], BF16, stack=st4) for _ in range(2)]
    wds = [b.sb([128, 1024], BF16, stack=st4) for _ in range(2)]
    combrep = b.sb([128, QW], F32, stack=st4)
    xfq = [b.sb([128, CW], F32, stack=st4) for _ in range(NK)]
    vtile = [b.sb([128, CW], F32, stack=st4) for _ in range(NK)]
    sqt = b.sb([128, CW], F32, stack=st4)
    rows = b.sb([1, 4 * CW], F32, stack=st4)
    ones_col = b.sb([128, 128], F32, stack=st4)
    t_ones = b.emit("vector", lambda h: h.memset(ones_col[:], 1.0))
    epsc = b.sb([128, 1], F32, stack=st4)
    b.emit("vector", lambda h: h.memset(epsc[:], 1e-5))
    q_free = None
    for q in range(NQ):
        lq = []
        for k in range(NK):
            lq.append(b.emit("sync", lambda h, k=k, q=q: h.dma_start(out=x3bq[k][:],
                      in_=x3b_h[k * 128:(k + 1) * 128, q * QW:(q + 1) * QW]), dma=True,
                      waits=[q_free]))
        mz = None
        for m2 in range(NK):
            mz = b.emit("vector", lambda h, m2=m2: h.memset(y_acc[m2][:], 0.0), waits=[q_free])
        wg_free = [None, None]; wu_free = [None, None]; wd_free = [None, None]
        e_done = None
        for e in range(17):
            shared = (e == 16)
            cwr_toks = []
            if not shared:
                for n2 in range(NC2):
                    bkc, bkcf = banks.get()
                    tt = mm_chain(b, bkc[:, 0:CW],
                                  [(sel16[:, e * 128:(e + 1) * 128],
                                    comb_cm[:, q * QW + n2 * CW:q * QW + (n2 + 1) * CW])],
                                  waits=[t_sel, t_cw], free_tok=bkcf)
                    ce = b.emit("vector", lambda h, bkc=bkc, n2=n2: h.tensor_copy(
                        out=combrep[:, n2 * CW:(n2 + 1) * CW], in_=bkc[:, 0:CW]), waits=[tt])
                    banks.set_free(bkc, ce)
                    cwr_toks.append(ce)
            h_tok = None
            for m in range(NK):
                s_ = m % 2
                lwg = b.emit("sync", lambda h, e=e, m=m, s_=s_: h.dma_start(out=wgs[s_][:],
                             in_=(sgb_h[m] if e == 16 else wgb_h[e, m])), dma=True,
                             waits=[wg_free[s_]])
                lwu = b.emit("sync", lambda h, e=e, m=m, s_=s_: h.dma_start(out=wus[s_][:],
                             in_=(sub_h[m] if e == 16 else wub_h[e, m])), dma=True,
                             waits=[wu_free[s_]])
                for n2 in range(NC2):
                    bkg, bkgf = banks.get()
                    mmg = mm_chain(b, bkg[:, 0:CW],
                                   [(wgs[s_][:, k * 128:(k + 1) * 128],
                                     x3bq[k][:, n2 * CW:(n2 + 1) * CW]) for k in range(NK)],
                                   waits=[lwg, lq[-1]], free_tok=bkgf)
                    bku, bkuf = banks.get()
                    mmu = mm_chain(b, bku[:, 0:CW],
                                   [(wus[s_][:, k * 128:(k + 1) * 128],
                                     x3bq[k][:, n2 * CW:(n2 + 1) * CW]) for k in range(NK)],
                                   waits=[lwu], free_tok=bkuf)
                    sg = b.emit("scalar", lambda h, bkg=bkg: h.activation(out=hgs[:],
                                in_=bkg[:, 0:CW], func=AF.Silu), waits=[mmg, h_tok])
                    banks.set_free(bkg, sg)
                    h_tok = b.emit("vector", lambda h, bku=bku, m=m, n2=n2: h.tensor_tensor(
                        out=hbf[m][:, n2 * CW:(n2 + 1) * CW], in0=hgs[:], in1=bku[:, 0:CW],
                        op=ALU.mult), waits=[mmu, sg])
                    banks.set_free(bku, h_tok)
                wg_free[s_] = h_tok; wu_free[s_] = h_tok
            y_tok = None
            for m2 in range(NK):
                s_ = m2 % 2
                lwd = b.emit("sync", lambda h, e=e, m2=m2, s_=s_: h.dma_start(out=wds[s_][:],
                             in_=(sdb_h[m2] if e == 16 else wdb_h[e, m2])), dma=True,
                             waits=[wd_free[s_]])
                for n2 in range(NC2):
                    bkd, bkdf = banks.get()
                    mmd = mm_chain(b, bkd[:, 0:CW],
                                   [(wds[s_][:, k2 * 128:(k2 + 1) * 128],
                                     hbf[k2][:, n2 * CW:(n2 + 1) * CW]) for k2 in range(NK)],
                                   waits=[lwd, h_tok], free_tok=bkdf)
                    if shared:
                        y_tok = b.emit("vector", lambda h, bkd=bkd, m2=m2, n2=n2:
                                       h.tensor_tensor(out=y_acc[m2][:, n2 * CW:(n2 + 1) * CW],
                                       in0=y_acc[m2][:, n2 * CW:(n2 + 1) * CW],
                                       in1=bkd[:, 0:CW], op=ALU.add), waits=[mmd, mz])
                        banks.set_free(bkd, y_tok)
                    else:
                        tv1 = b.emit("vector", lambda h, bkd=bkd, n2=n2: h.tensor_tensor(
                            out=sqt[:], in0=bkd[:, 0:CW],
                            in1=combrep[:, n2 * CW:(n2 + 1) * CW], op=ALU.mult),
                            waits=[mmd, cwr_toks[n2]])
                        banks.set_free(bkd, tv1)
                        y_tok = b.emit("vector", lambda h, m2=m2, n2=n2: h.tensor_tensor(
                            out=y_acc[m2][:, n2 * CW:(n2 + 1) * CW],
                            in0=y_acc[m2][:, n2 * CW:(n2 + 1) * CW], in1=sqt[:], op=ALU.add),
                            waits=[mz])
                wd_free[s_] = y_tok
            e_done = y_tok
        ln_toks = []
        for n2 in range(NC2):
            lxf = []
            for k in range(NK):
                lxf.append(b.emit("sync", lambda h, k=k, q=q, n2=n2: h.dma_start(
                    out=xfq[k][:], in_=x3T[k * 128:(k + 1) * 128,
                                          q * QW + n2 * CW:q * QW + (n2 + 1) * CW]),
                    dma=True, waits=[q_free, ln_toks[-1] if ln_toks else None]))
            ut = None
            for m2 in range(NK):
                ut = b.emit("vector", lambda h, m2=m2, n2=n2: h.tensor_tensor(
                    out=vtile[m2][:], in0=y_acc[m2][:, n2 * CW:(n2 + 1) * CW],
                    in1=xfq[m2][:], op=ALU.add), waits=[lxf[m2], e_done])
            vt = ln_chunk(b, st4, banks, [v_[:] for v_ in vtile], CW, ones_col,
                          epsc, [v_[:] for v_ in vtile], affine=None,
                          waits=[ut, t_ones], sq=sqt, rows=rows)
            sts = []
            for m2 in range(NK):
                sts.append(b.emit("gpsimd", lambda h, m2=m2, q=q, n2=n2: h.dma_start(
                    out=outT[m2 * 128:(m2 + 1) * 128, q * QW + n2 * CW:q * QW + (n2 + 1) * CW],
                    in_=vtile[m2][:]), dma=True, waits=[vt]))
            ln_toks.append(sts[-1])
        q_free = [ln_toks[-1], b.tok("tensor")]
    st4.close()
    b.barrier(); banks.reset()
    return b.tok("gpsimd", dma=True)


def _prep_l1_ffn(inputs):
    f = np.float32
    c = _consts()
    d = {}
    d["gwT"] = np.ascontiguousarray(np.asarray(inputs["L1_ffn_gate_w"], f).T)
    d["ebias"] = np.asarray(inputs["L1_ffn_expert_bias"], f).reshape(1, 16).copy()
    for nm, key in (("wgr", "w_gate"), ("wur", "w_up"), ("wdr", "w_down")):
        w = np.asarray(inputs["L1_ffn_" + key], f)
        d[nm] = np.ascontiguousarray(w.reshape(16, 8, 128, 8, 128).transpose(0, 1, 4, 3, 2)
                                     .reshape(16, 8, 128, 1024))
    for nm, key in (("sgr", "shared_gate_w"), ("sur", "shared_up_w"), ("sdr", "shared_down_w")):
        w = np.asarray(inputs["L1_ffn_" + key], f)
        d[nm] = np.ascontiguousarray(w.reshape(8, 128, 8, 128).transpose(0, 3, 2, 1)
                                     .reshape(8, 128, 1024))
    d["cid"] = c["cid"]
    d["csel16"] = c["csel16"]
    return d


def stage_l1_attn(b, tn, banks):
    st = contextlib.ExitStack()
    x2T = tn("x2T", [D, T], F32, "in")
    x3T = tn("x3T", [D, T], F32, "out_x3")
    x2b_h = tn("x2b_h", [D, T], BF16, "Internal")
    qn_h = tn("qn_h", [D, T], BF16, "Internal")
    kn_h = tn("kn_h", [D, T], BF16, "Internal")
    v_h = tn("v_h", [D, T], BF16, "Internal")
    og_h = tn("og_h", [D, T], BF16, "Internal")
    o_h = tn("o_h", [D, T], F32, "Internal")
    cid_h = tn("cid", [128, 128], F32, "in")
    chead_h = tn("chead", [16, 8 * 128], F32, "in")
    sel16_h = tn("csel16", [16, 2048], F32, "in")
    celast_h = tn("celast", [128, 1], F32, "in")
    mUS_h = tn("mUS", [128, 128], F32, "in")
    mUI_h = tn("mUI", [128, 128], F32, "in")
    mLS_h = tn("mLS", [128, 128], F32, "in")

    arow = b.sb([16, T], F32, stack=st)
    brow = b.sb([16, T], F32, stack=st)
    cid = b.sb([128, 128], F32, stack=st)
    t_cid = b.emit("sync", lambda h: h.dma_start(out=cid[:], in_=cid_h), dma=True)
    t6 = banks.t[6]; t7 = banks.t[7]
    banks = Banks(b, banks.t[:6])
    t6_free = [None]; t7_free = [None]
    bo16 = []
    t_bo16 = None
    for tt in range(NK):
        m_ = b.sb([128, 16], F32, stack=st)
        b.emit("vector", lambda h, m_=m_: h.memset(m_[:], 0.0))
        b.emit("vector", lambda h, m_=m_, tt=tt: h.memset(m_[0:64, 2 * tt:2 * tt + 1], 1.0))
        t_bo16 = b.emit("vector", lambda h, m_=m_, tt=tt: h.memset(m_[64:128, 2 * tt + 1:2 * tt + 2], 1.0))
        bo16.append(m_)
    chead = b.sb([16, 8 * 128], F32, stack=st)
    t_chead = b.emit("sync", lambda h: h.dma_start(out=chead[:], in_=chead_h), dma=True)
    acols = [b.sb([128, 1], F32, stack=st) for _ in range(NK)]
    bcols2 = [b.sb([128, 1], F32, stack=st) for _ in range(NK)]

    stP = contextlib.ExitStack()
    wq, twq = cast_resident2(b, stP, tn("l1wq", [D, D], F32, "in"))
    wk, twk = cast_resident2(b, stP, tn("l1wk", [D, D], F32, "in"))
    wv, twv = cast_resident2(b, stP, tn("l1wv", [D, D], F32, "in"))
    wg, twg = cast_resident2(b, stP, tn("l1wg", [D, D], F32, "in"))
    wa = []; wb = []
    t_wab = None
    for k in range(NK):
        ca, _ = load_col(b, stP, tn("l1wa", [D, 16], F32, "in")[k * 128:(k + 1) * 128, :], w=16)
        cb, t_wab = load_col(b, stP, tn("l1wb", [D, 16], F32, "in")[k * 128:(k + 1) * 128, :], w=16)
        wa.append(ca); wb.append(cb)
    bacol = b.sb([128, 2], F32, stack=stP)
    t_bac = b.emit("sync", lambda h: h.dma_start(out=bacol[0:16, 0:1],
                   in_=tn("l1ba", [16, 1], F32, "in")), dma=True)
    t_bac = b.emit("sync", lambda h: h.dma_start(out=bacol[0:16, 1:2],
                   in_=tn("l1bb", [16, 1], F32, "in")), dma=True)
    bcols = {}
    for nm in ("bq", "bk", "bv", "bg"):
        cols = []
        for k in range(NK):
            c, t_bc = load_col(b, stP, tn("l1" + nm, [NK, 128, 1], F32, "in")[k])
            cols.append(c)
        bcols[nm] = cols
    ccols = {}
    for nm in ("qc", "kc", "vc"):
        cols = []
        for k in range(NK):
            c, t_cc = load_col(b, stP, tn("l1" + nm, [NK, 128, 4], F32, "in")[k], w=4)
            cols.append(c)
        ccols[nm] = cols
    xf = [b.sb([128, CW], F32, stack=stP) for _ in range(NK)]
    xb = [b.sb([128, CW], BF16, stack=stP) for _ in range(NK)]
    qext = [b.sb([128, CW + 3], F32, stack=stP) for _ in range(NK)]
    tails = {pn: [b.sb([128, 3], F32, stack=stP) for _ in range(NK)] for pn in ("q", "k", "v")}
    qs = [b.sb([128, CW], F32, stack=stP) for _ in range(NK)]
    cvt = b.sb([128, CW], F32, stack=stP)
    sq2 = b.sb([128, CW], F32, stack=stP)
    nrm = b.sb([16, CW], F32, stack=stP)
    obf = b.sb([128, CW], BF16, stack=stP)
    for pn in ("q", "k", "v"):
        for m in range(NK):
            b.emit("vector", lambda h, pn=pn, m=m: h.memset(tails[pn][m][:], 0.0))
    t_tz = b.tok("vector")
    buf_free = None
    for n in range(NTOK):
        lts = []
        for k in range(NK):
            lts.append(b.emit("sync", lambda h, k=k, n=n: h.dma_start(out=xf[k][:],
                       in_=x2T[k * 128:(k + 1) * 128, n * CW:(n + 1) * CW]), dma=True,
                       waits=[buf_free]))
        cbt = []
        for k in range(NK):
            cbt.append(b.emit("vector", lambda h, k=k: h.tensor_copy(out=xb[k][:], in_=xf[k][:]),
                       waits=[lts[k], buf_free]))
        prev_st = None
        for k in range(NK):
            prev_st = b.emit("gpsimd", lambda h, k=k, n=n: h.dma_start(
                out=x2b_h[k * 128:(k + 1) * 128, n * CW:(n + 1) * CW], in_=xb[k][:]),
                dma=True, waits=[cbt[k]])
        bka, bkaf = banks.get()
        mma = mm_chain(b, bka[0:16, 0:CW], [(wa[k][:], xf[k][:]) for k in range(NK)],
                       waits=[t_wab, lts[-1]], free_tok=bkaf)
        eva = b.emit("scalar", lambda h, bka=bka, n=n: h.activation(
            out=arow[:, n * CW:(n + 1) * CW], in_=bka[0:16, 0:CW], func=AF.Sigmoid,
            bias=bacol[0:16, 0:1]), waits=[mma, t_bac])
        banks.set_free(bka, eva)
        bkb, bkbf = banks.get()
        mmb = mm_chain(b, bkb[0:16, 0:CW], [(wb[k][:], xf[k][:]) for k in range(NK)],
                       waits=[t_wab], free_tok=bkbf)
        evb = b.emit("scalar", lambda h, bkb=bkb, n=n: h.activation(
            out=brow[:, n * CW:(n + 1) * CW], in_=bkb[0:16, 0:CW], func=AF.Sigmoid,
            bias=bacol[0:16, 1:2]), waits=[mmb])
        banks.set_free(bkb, evb)
        og_tok = None
        for m in range(NK):
            bko, bkof = banks.get()
            mmo = mm_chain(b, bko[:, 0:CW], [(wg[k][:, m * 128:(m + 1) * 128], xb[k][:])
                                             for k in range(NK)],
                           waits=[twg, cbt[-1]], free_tok=bkof)
            so = b.emit("scalar", lambda h, bko=bko, m=m: h.activation(out=obf[:],
                        in_=bko[:, 0:CW], func=AF.Silu, bias=bcols["bg"][m][:]),
                        waits=[mmo, og_tok])
            banks.set_free(bko, so)
            og_tok = b.emit("gpsimd", lambda h, m=m, n=n: h.dma_start(
                out=og_h[m * 128:(m + 1) * 128, n * CW:(n + 1) * CW], in_=obf[:]),
                dma=True, waits=[so])
        pst = None
        for pn, wt, twt, bnm, cnm, outh in (("q", wq, twq, "bq", "qc", qn_h),
                                            ("k", wk, twk, "bk", "kc", kn_h),
                                            ("v", wv, twv, "bv", "vc", v_h)):
            stat_tok = None
            for m in range(NK):
                bk1, bk1f = banks.get()
                mmt = mm_chain(b, bk1[:, 0:CW], [(wt[k][:, m * 128:(m + 1) * 128], xb[k][:])
                                                 for k in range(NK)],
                               waits=[twt, cbt[-1]], free_tok=bk1f)
                tc = b.emit("vector", lambda h, pn=pn, m=m: h.tensor_copy(
                    out=qext[m][:, 0:3], in_=tails[pn][m][:]), waits=[t_tz])
                bcol_ = bcols[bnm][m]
                el = b.emit("vector", lambda h, bk1=bk1, m=m, bcol_=bcol_: h.tensor_scalar(
                    out=qext[m][:, 3:CW + 3], in0=bk1[:, 0:CW], scalar1=bcol_[:],
                    scalar2=None, op0=ALU.add), waits=[mmt])
                banks.set_free(bk1, el)
                tl = b.emit("vector", lambda h, pn=pn, m=m: h.tensor_copy(
                    out=tails[pn][m][:], in_=qext[m][:, CW:CW + 3]))
                cc = ccols[cnm][m]
                v1 = b.emit("vector", lambda h, m=m, cc=cc: h.tensor_scalar(out=cvt[:],
                            in0=qext[m][:, 0:CW], scalar1=cc[:, 0:1], scalar2=None,
                            op0=ALU.mult))
                for dk in range(1, 4):
                    v1 = b.emit("vector", lambda h, m=m, cc=cc, dk=dk: h.scalar_tensor_tensor(
                        out=cvt[:], in0=qext[m][:, dk:CW + dk], scalar=cc[:, dk:dk + 1],
                        in1=cvt[:], op0=ALU.mult, op1=ALU.add))
                sil = b.emit("scalar", lambda h, m=m: h.activation(out=qs[m][:], in_=cvt[:],
                             func=AF.Silu), waits=[v1])
                if pn in ("q", "k"):
                    sqr = b.emit("scalar", lambda h, m=m: h.activation(out=sq2[:],
                                 in_=qs[m][:], func=AF.Square), waits=[sil, stat_tok])
                    stat_tok = mm_chain(b, t6[0:16, 0:CW], [(bo16[m][:], sq2[:])],
                                        waits=[sqr, t_bo16], first=(m == 0), last=(m == NK - 1),
                                        free_tok=t6_free[0] if m == 0 else None)
                else:
                    stat_tok = sil
            if pn in ("q", "k"):
                s1 = b.emit("scalar", lambda h: h.activation(out=nrm[:],
                            in_=t6[0:16, 0:CW], func=AF.Sqrt), waits=[stat_tok])
                t6_free[0] = s1
                s2 = b.emit("vector", lambda h: h.tensor_scalar(out=nrm[:], in0=nrm[:],
                            scalar1=1e-6, scalar2=None, op0=ALU.add), waits=[s1])
                s3 = b.emit("vector", lambda h: h.reciprocal(out=nrm[:], in_=nrm[:]))
                for m in range(NK):
                    bk2, bk2f = banks.get()
                    mmr = mm_chain(b, bk2[:, 0:CW], [(chead[:, m * 128:(m + 1) * 128], nrm[:])],
                                   waits=[s3, t_chead], free_tok=bk2f)
                    nq = b.emit("vector", lambda h, bk2=bk2, m=m: h.tensor_tensor(out=obf[:],
                                in0=qs[m][:], in1=bk2[:, 0:CW], op=ALU.mult),
                                waits=[mmr, pst])
                    banks.set_free(bk2, nq)
                    pst = b.emit("gpsimd", lambda h, m=m, n=n, outh=outh: h.dma_start(
                        out=outh[m * 128:(m + 1) * 128, n * CW:(n + 1) * CW], in_=obf[:]),
                        dma=True, waits=[nq])
            else:
                for m in range(NK):
                    nv = b.emit("vector", lambda h, m=m: h.tensor_copy(out=obf[:],
                                in_=qs[m][:]), waits=[pst, stat_tok])
                    pst = b.emit("gpsimd", lambda h, m=m, n=n: h.dma_start(
                        out=v_h[m * 128:(m + 1) * 128, n * CW:(n + 1) * CW], in_=obf[:]),
                        dma=True, waits=[nv])
        buf_free = [b.tok("tensor"), pst, og_tok]
    stP.close()
    b.barrier(); banks.reset()
    t6_free[0] = None; t7_free[0] = None

    stS = contextlib.ExitStack()
    C = 128
    NCH = T // C
    sel16 = b.sb([16, 2048], F32, stack=stS)
    t_sel = b.emit("sync", lambda h: h.dma_start(out=sel16[:], in_=sel16_h), dma=True)
    celast = b.sb([128, 1], F32, stack=stS)
    t_cel = b.emit("sync", lambda h: h.dma_start(out=celast[:], in_=celast_h), dma=True)
    masks = {}
    t_msk = None
    for nm, hh in (("US", mUS_h), ("UI", mUI_h), ("LS", mLS_h)):
        mm_ = b.sb([128, 128], F32, stack=stS)
        t_msk = b.emit("sync", lambda h, mm_=mm_, hh=hh: h.dma_start(out=mm_[:], in_=hh), dma=True)
        masks[nm] = mm_
    ones_row = b.sb([1, 128], F32, stack=stS)
    t_onr = b.emit("vector", lambda h: h.memset(ones_row[:], 1.0))
    lna = b.sb([16, T], F32, stack=stS)
    t_lna = b.emit("scalar", lambda h: h.activation(out=lna[:], in_=arow[:], func=AF.Ln))
    zeros16 = b.sb([16, 128], F32, stack=stS)
    t_z16 = b.emit("vector", lambda h: h.memset(zeros16[:], 0.0))
    Stf2 = [b.sb([128, 64], F32, stack=stS) for _ in range(NK)]
    Stb2 = [b.sb([128, 64], BF16, stack=stS) for _ in range(NK)]
    t_st0 = None
    for tt in range(NK):
        b.emit("vector", lambda h, tt=tt: h.memset(Stf2[tt][:], 0.0))
        t_st0 = b.emit("vector", lambda h, tt=tt: h.memset(Stb2[tt][:], 0.0))
    g_sb = b.sb([16, 128], F32, stack=stS)
    gam_cm = b.sb([16, 128], F32, stack=stS)
    gT = b.sb([128, 16], F32, stack=stS)
    bT = b.sb([128, 16], F32, stack=stS)
    glr = b.sb([1, 16], F32, stack=stS)
    glrep = b.sb([128, 16], F32, stack=stS)
    eglg = b.sb([128, 16], F32, stack=stS)
    elast = b.sb([128, 16], F32, stack=stS)
    kcb = [b.sb([128, C], BF16, stack=stS) for _ in range(NK)]
    qcb = [b.sb([128, C], BF16, stack=stS) for _ in range(NK)]
    ktl = [b.sb([128, C], BF16, stack=stS) for _ in range(NK)]
    qtl = [b.sb([128, C], BF16, stack=stS) for _ in range(NK)]
    svtok = b.sb([128, 64], BF16, stack=stS)
    sktok = b.sb([128, 64], BF16, stack=stS)
    sD = b.sb([128, C], F32, stack=stS)
    sE = b.sb([128, C], F32, stack=stS)
    sMi = b.sb([128, C], F32, stack=stS)
    sMt = b.sb([128, C], BF16, stack=stS)
    sMs = b.sb([128, C], F32, stack=stS)
    sT1 = b.sb([128, C], F32, stack=stS)
    sNt = [b.sb([128, C], BF16, stack=stS) for _ in range(2)]
    sN = [b.sb([128, C], BF16, stack=stS) for _ in range(2)]
    sX = b.sb([128, 64], F32, stack=stS)
    sXb = b.sb([128, 64], BF16, stack=stS)
    sW_ = b.sb([128, 64], F32, stack=stS)
    sO = b.sb([128, 64], F32, stack=stS)
    sUb = b.sb([128, 64], BF16, stack=stS)
    o_cm = [b.sb([128, C], F32, stack=stS) for _ in range(NK)]
    osq = b.sb([128, C], F32, stack=stS)
    chunk_free = None
    stat_tok6 = None
    stat_tok7 = None
    for c in range(NCH):
        cs = slice(c * C, (c + 1) * C)
        sc1 = b.emit("vector", lambda h, cs=cs: h.tensor_tensor_scan(out=g_sb[:],
                     data0=lna[:, cs], data1=zeros16[:], initial=0.0, op0=ALU.add,
                     op1=ALU.add), waits=[t_lna, t_z16, chunk_free])
        sc2 = b.emit("scalar", lambda h: h.activation(out=gam_cm[:], in_=g_sb[:], func=AF.Exp),
                     waits=[sc1])
        bkg, bkgf = banks.get()
        tgT = mm_chain(b, bkg[0:128, 0:16], [(g_sb[:], cid[0:16, 0:16])],
                       waits=[sc1, t_cid], free_tok=bkgf)
        egT = b.emit("vector", lambda h, bkg=bkg: h.tensor_copy(out=gT[:], in_=bkg[:, 0:16]),
                     waits=[tgT])
        banks.set_free(bkg, egT)
        bkb2, bkb2f = banks.get()
        tbT = mm_chain(b, bkb2[0:128, 0:16], [(brow[:, cs], cid[0:16, 0:16])],
                       waits=[chunk_free], free_tok=bkb2f)
        ebT = b.emit("vector", lambda h, bkb2=bkb2: h.tensor_copy(out=bT[:], in_=bkb2[:, 0:16]),
                     waits=[tbT])
        banks.set_free(bkb2, ebT)
        bkl, bklf = banks.get()
        tgl = mm_chain(b, bkl[0:1, 0:16], [(celast[:], gT[:])], waits=[egT, t_cel],
                       free_tok=bklf)
        egl = b.emit("vector", lambda h, bkl=bkl: h.tensor_copy(out=glr[:], in_=bkl[0:1, 0:16]),
                     waits=[tgl])
        banks.set_free(bkl, egl)
        bkr, bkrf = banks.get()
        tgr = mm_chain(b, bkr[0:128, 0:16], [(ones_row[:], glr[:])], waits=[egl, t_onr],
                       free_tok=bkrf)
        egr = b.emit("vector", lambda h, bkr=bkr: h.tensor_copy(out=glrep[:], in_=bkr[:, 0:16]),
                     waits=[tgr])
        banks.set_free(bkr, egr)
        v1 = b.emit("vector", lambda h: h.tensor_tensor(out=eglg[:], in0=glrep[:], in1=gT[:],
                    op=ALU.subtract), waits=[egr])
        v2 = b.emit("scalar", lambda h: h.activation(out=eglg[:], in_=eglg[:], func=AF.Exp),
                    waits=[v1])
        v3 = b.emit("scalar", lambda h: h.activation(out=elast[:], in_=glrep[:], func=AF.Exp),
                    waits=[egr])
        lks = []
        for tt2 in range(NK):
            lks.append(b.emit("sync", lambda h, tt2=tt2, cs=cs: h.dma_start(out=kcb[tt2][:],
                       in_=kn_h[tt2 * 128:(tt2 + 1) * 128, cs]), dma=True, waits=[chunk_free]))
            lks.append(b.emit("sync", lambda h, tt2=tt2, cs=cs: h.dma_start(out=qcb[tt2][:],
                       in_=qn_h[tt2 * 128:(tt2 + 1) * 128, cs]), dma=True))
        tl_tok = None
        for tt2 in range(NK):
            bkt, bktf = banks.get()
            tgam = mm_chain(b, bkt[:, 0:C], [(chead[:, tt2 * 128:(tt2 + 1) * 128], gam_cm[:])],
                            waits=[sc2, t_chead], free_tok=bktf)
            k1 = b.emit("vector", lambda h, bkt=bkt, tt2=tt2: h.tensor_tensor(out=ktl[tt2][:],
                        in0=kcb[tt2][:], in1=bkt[:, 0:C], op=ALU.mult),
                        waits=[tgam, lks[2 * tt2]])
            tl_tok = b.emit("vector", lambda h, bkt=bkt, tt2=tt2: h.tensor_tensor(
                out=qtl[tt2][:], in0=qcb[tt2][:], in1=bkt[:, 0:C], op=ALU.mult),
                waits=[lks[2 * tt2 + 1]])
            banks.set_free(bkt, tl_tok)
        head_gate = None
        for hh in range(NH):
            tt2 = hh // 2
            half = hh % 2
            sl = slice(64 * half, 64 * half + 64)
            kc_h = kcb[tt2][sl, :]
            qc_h = qcb[tt2][sl, :]
            ktl_h = ktl[tt2][sl, :]
            qtl_h = qtl[tt2][sl, :]
            Stf_h = Stf2[tt2][sl, :]
            Stb_h = Stb2[tt2][sl, :]
            bkK, bkKf = banks.get()
            tK = mm_chain(b, bkK[:, 0:C], [(kc_h, kc_h)], waits=[lks[-1], tl_tok],
                          free_tok=bkKf)
            bkQ, bkQf = banks.get()
            tKq = mm_chain(b, bkQ[:, 0:C], [(kc_h, qc_h)], free_tok=bkQf)
            bkB, bkBf = banks.get()
            tB = mm_chain(b, bkB[:, 0:C], [(sel16[:, hh * 128:(hh + 1) * 128], brow[:, cs])],
                          waits=[t_sel, chunk_free], free_tok=bkBf)
            bkG, bkGf = banks.get()
            tG = mm_chain(b, bkG[:, 0:C], [(sel16[:, hh * 128:(hh + 1) * 128], g_sb[:])],
                          waits=[sc1], free_tok=bkGf)
            d1 = b.emit("vector", lambda h, bkG=bkG, hh=hh: h.tensor_scalar(out=sD[:],
                        in0=bkG[:, 0:C], scalar1=gT[:, hh:hh + 1], scalar2=0.0,
                        op0=ALU.subtract, op1=ALU.min), waits=[tG, egT, head_gate])
            d2 = b.emit("scalar", lambda h: h.activation(out=sE[:], in_=sD[:], func=AF.Exp),
                        waits=[d1])
            d3 = b.emit("gpsimd", lambda h: h.tensor_tensor(out=sMi[:], in0=sE[:],
                        in1=masks["UI"][:], op=ALU.mult), waits=[d2, t_msk])
            d4 = b.emit("vector", lambda h, bkQ=bkQ: h.tensor_tensor(out=sMt[:], in0=sMi[:],
                        in1=bkQ[:, 0:C], op=ALU.mult), waits=[tKq, d3])
            banks.set_free(bkQ, d4)
            d5 = b.emit("gpsimd", lambda h: h.tensor_tensor(out=sMs[:], in0=sE[:],
                        in1=masks["US"][:], op=ALU.mult), waits=[d2])
            d6_ = b.emit("vector", lambda h, bkK=bkK: h.tensor_tensor(out=sT1[:], in0=sMs[:],
                         in1=bkK[:, 0:C], op=ALU.mult), waits=[tK, d5])
            nt_idx = hh % 2
            d7 = b.emit("vector", lambda h, bkB=bkB, nt_idx=nt_idx: h.scalar_tensor_tensor(
                out=sNt[nt_idx][:], in0=sT1[:], scalar=-1.0, in1=bkB[:, 0:C], op0=ALU.mult,
                op1=ALU.mult), waits=[tB, d6_])
            d8 = b.emit("vector", lambda h, bkG=bkG, hh=hh: h.tensor_scalar(out=sD[:],
                        in0=bkG[:, 0:C], scalar1=gT[:, hh:hh + 1], scalar2=-1.0,
                        op0=ALU.subtract, op1=ALU.mult), waits=[d1])
            banks.set_free(bkG, d8)
            d8 = b.emit("vector", lambda h: h.tensor_single_scalar(out=sD[:], in_=sD[:],
                        scalar=0.0, op=ALU.min))
            d9 = b.emit("scalar", lambda h: h.activation(out=sE[:], in_=sD[:], func=AF.Exp),
                        waits=[d8, d5])
            d10 = b.emit("gpsimd", lambda h: h.tensor_tensor(out=sMs[:], in0=sE[:],
                         in1=masks["LS"][:], op=ALU.mult), waits=[d9])
            d11 = b.emit("vector", lambda h, bkK=bkK: h.tensor_tensor(out=sT1[:], in0=sMs[:],
                         in1=bkK[:, 0:C], op=ALU.mult), waits=[d10])
            banks.set_free(bkK, d11)
            d12 = b.emit("vector", lambda h, hh=hh, nt_idx=nt_idx: h.tensor_scalar(
                out=sN[nt_idx][:], in0=sT1[:], scalar1=bT[:, hh:hh + 1], scalar2=-1.0,
                op0=ALU.mult, op1=ALU.mult), waits=[ebT])
            banks.set_free(bkB, d12)
            lv_ = b.emit("sync", lambda h, tt2=tt2, half=half, cs=cs: h.dma_start_transpose(
                out=svtok[:], in_=v_h[tt2 * 128 + 64 * half:tt2 * 128 + 64 * half + 64, cs]),
                dma=True, waits=[chunk_free, head_gate])
            lk_ = b.emit("sync", lambda h, tt2=tt2, half=half, cs=cs: h.dma_start_transpose(
                out=sktok[:], in_=kn_h[tt2 * 128 + 64 * half:tt2 * 128 + 64 * half + 64, cs]),
                dma=True)
            bkS, bkSf = banks.get()
            tSK = mm_chain(b, bkS[:, 0:64], [(ktl_h, Stb_h)], waits=[tl_tok, t_st0],
                           free_tok=bkSf)
            w1_ = b.emit("vector", lambda h, bkS=bkS: h.scalar_tensor_tensor(out=sW_[:],
                         in0=bkS[:, 0:64], scalar=-1.0, in1=svtok[:], op0=ALU.mult,
                         op1=ALU.add), waits=[tSK, lv_])
            banks.set_free(bkS, w1_)
            w2_ = b.emit("vector", lambda h, hh=hh: h.tensor_scalar(out=sX[:], in0=sW_[:],
                         scalar1=bT[:, hh:hh + 1], scalar2=None, op0=ALU.mult))
            Rcur = sN[nt_idx]; Tcur = sNt[nt_idx]
            r_tok = d12; t_tok = d7
            x_tok = w2_
            sR2 = b.sb([128, C], BF16, stack=stS) if c == 0 and hh == 0 else sR2
            sT2b = b.sb([128, C], BF16, stack=stS) if c == 0 and hh == 0 else sT2b
            for lv in range(7):
                xb_ = b.emit("scalar", lambda h: h.copy(out=sXb[:], in_=sX[:]), waits=[x_tok])
                bkA, bkAf = banks.get()
                tA = mm_chain(b, bkA[:, 0:64], [(Tcur[:], sXb[:])], waits=[xb_, t_tok],
                              free_tok=bkAf)
                x_tok = b.emit("vector", lambda h, bkA=bkA: h.tensor_tensor(out=sX[:],
                               in0=sX[:], in1=bkA[:, 0:64], op=ALU.add), waits=[tA])
                banks.set_free(bkA, x_tok)
                if lv < 6:
                    bkP, bkPf = banks.get()
                    tP = mm_chain(b, bkP[:, 0:C], [(Tcur[:], Rcur[:])], waits=[r_tok, t_tok],
                                  free_tok=bkPf)
                    nR = sR2 if (lv % 2 == 0) else sN[nt_idx]
                    r_new = b.emit("scalar", lambda h, bkP=bkP, nR=nR: h.copy(out=nR[:],
                                   in_=bkP[:, 0:C]), waits=[tP, r_tok])
                    banks.set_free(bkP, r_new)
                    bkP2, bkP2f = banks.get()
                    tP2 = mm_chain(b, bkP2[:, 0:C], [(Rcur[:], Tcur[:])], free_tok=bkP2f)
                    nT = sT2b if (lv % 2 == 0) else sNt[nt_idx]
                    t_new = b.emit("scalar", lambda h, bkP2=bkP2, nT=nT: h.copy(out=nT[:],
                                   in_=bkP2[:, 0:C]), waits=[tP2, t_tok])
                    banks.set_free(bkP2, t_new)
                    Rcur, Tcur = nR, nT
                    r_tok, t_tok = r_new, t_new
            xbf = b.emit("scalar", lambda h: h.copy(out=sXb[:], in_=sX[:]), waits=[x_tok])
            bkO, bkOf = banks.get()
            tO = mm_chain(b, bkO[:, 0:64], [(sMt[:], sXb[:])], waits=[xbf, d4], free_tok=bkOf,
                          last=False)
            tO2 = mm_chain(b, bkO[:, 0:64], [(qtl_h, Stb_h)], first=False)
            eO = b.emit("scalar", lambda h, bkO=bkO: h.copy(out=sO[:], in_=bkO[:, 0:64]),
                        waits=[tO2])
            banks.set_free(bkO, eO)
            bkT, bkTf = banks.get()
            tOT = b.emit("tensor", lambda h, bkT=bkT: h.transpose(out=bkT[0:64, 0:128],
                         in_=sO[:], identity=cid[:]), waits=[eO, bkTf, t_cid])
            eOT = b.emit("vector", lambda h, bkT=bkT, tt2=tt2, sl=sl: h.tensor_copy(
                out=o_cm[tt2][sl, :], in_=bkT[0:64, 0:128]), waits=[tOT])
            banks.set_free(bkT, eOT)
            u1 = b.emit("vector", lambda h, hh=hh: h.tensor_scalar(out=sUb[:], in0=sX[:],
                        scalar1=eglg[:, hh:hh + 1], scalar2=None, op0=ALU.mult),
                        waits=[v2, xbf])
            bkD, bkDf = banks.get()
            tD = mm_chain(b, bkD[0:64, 0:64], [(sktok[:], sUb[:])], waits=[u1, lk_],
                          free_tok=bkDf)
            u2 = b.emit("vector", lambda h, bkD=bkD, hh=hh, Stf_h=Stf_h, sl=sl:
                h.scalar_tensor_tensor(out=Stf_h, in0=Stf_h, scalar=elast[sl, hh:hh + 1],
                in1=bkD[0:64, 0:64], op0=ALU.mult, op1=ALU.add), waits=[tD, v3, tO2])
            banks.set_free(bkD, u2)
            u3 = b.emit("scalar", lambda h, Stb_h=Stb_h, Stf_h=Stf_h: h.copy(out=Stb_h,
                        in_=Stf_h), waits=[u2])
            head_gate = [b.tok("vector"), b.tok("scalar"), b.tok("gpsimd"), b.tok("tensor")]
        st_toks = []
        for tt2 in range(NK):
            sq_ = b.emit("scalar", lambda h, tt2=tt2: h.activation(out=osq[:],
                         in_=o_cm[tt2][:], func=AF.Square), waits=[b.tok("vector"), stat_tok7])
            stat_tok6 = mm_chain(b, t6[0:16, 0:C], [(bo16[tt2][:], o_cm[tt2][:])],
                                 waits=[t_bo16, b.tok("vector")],
                                 first=(c == 0 and tt2 == 0), last=(c == NCH - 1 and tt2 == NK - 1),
                                 free_tok=None)
            stat_tok7 = mm_chain(b, t7[0:16, 0:C], [(bo16[tt2][:], osq[:])],
                                 waits=[sq_], first=(c == 0 and tt2 == 0),
                                 last=(c == NCH - 1 and tt2 == NK - 1), free_tok=None)
            st_toks.append(b.emit("gpsimd", lambda h, tt2=tt2, cs=cs: h.dma_start(
                out=o_h[tt2 * 128:(tt2 + 1) * 128, cs], in_=o_cm[tt2][:]),
                dma=True, waits=[stat_tok6]))
        chunk_free = [b.tok("tensor"), b.tok("vector"), b.tok("scalar"), st_toks[-1]]
    t_stats = [stat_tok6, stat_tok7]

    rows16 = b.sb([16, 8], F32, stack=stS)
    r1 = b.emit("vector", lambda h: h.tensor_reduce(out=rows16[:, 0:1], in_=t6[0:16, 0:C],
                axis=AX.X, op=ALU.add), waits=t_stats)
    r2 = b.emit("vector", lambda h: h.tensor_reduce(out=rows16[:, 1:2], in_=t7[0:16, 0:C],
                axis=AX.X, op=ALU.add))
    r3 = b.emit("vector", lambda h: h.tensor_scalar(out=rows16[:, 2:3], in0=rows16[:, 0:1],
                scalar1=1.0 / (T * HD), scalar2=None, op0=ALU.mult))
    r4 = b.emit("vector", lambda h: h.tensor_scalar(out=rows16[:, 3:4], in0=rows16[:, 1:2],
                scalar1=1.0 / (T * HD), scalar2=None, op0=ALU.mult))
    r5 = b.emit("vector", lambda h: h.tensor_tensor(out=rows16[:, 4:5], in0=rows16[:, 2:3],
                in1=rows16[:, 2:3], op=ALU.mult))
    r6 = b.emit("vector", lambda h: h.tensor_tensor(out=rows16[:, 4:5], in0=rows16[:, 3:4],
                in1=rows16[:, 4:5], op=ALU.subtract))
    epsS = b.sb([16, 1], F32, stack=stS)
    b.emit("vector", lambda h: h.memset(epsS[:], 1e-5))
    r7 = b.emit("scalar", lambda h: h.activation(out=rows16[:, 4:5], in_=rows16[:, 4:5],
                func=AF.Sqrt, bias=epsS[:]), waits=[r6])
    r8 = b.emit("vector", lambda h: h.reciprocal(out=rows16[:, 5:6], in_=rows16[:, 4:5]),
                waits=[r7])
    mv16 = b.sb([16, 2], F32, stack=stS)
    r9 = b.emit("vector", lambda h: h.tensor_copy(out=mv16[:, 0:1], in_=rows16[:, 2:3]))
    r10 = b.emit("vector", lambda h: h.tensor_copy(out=mv16[:, 1:2], in_=rows16[:, 5:6]))
    gnsc = []
    gnbi = []
    for k in range(NK):
        c1_, _ = load_col(b, stS, tn("l1gnsc", [NK, 128, 1], F32, "in")[k])
        c2_, t_gn = load_col(b, stS, tn("l1gnbi", [NK, 128, 1], F32, "in")[k])
        gnsc.append(c1_); gnbi.append(c2_)
    for k in range(NK):
        bkm, bkmf = banks.get()
        tmv = mm_chain(b, bkm[:, 0:2], [(chead[:, k * 128:(k + 1) * 128], mv16[:])],
                       waits=[r10, t_chead], free_tok=bkmf)
        e1_ = b.emit("vector", lambda h, bkm=bkm, k=k: h.tensor_tensor(out=acols[k][:],
                     in0=bkm[:, 1:2], in1=gnsc[k][:], op=ALU.mult), waits=[tmv, t_gn])
        e2_ = b.emit("vector", lambda h, bkm=bkm, k=k: h.tensor_scalar(out=bcols2[k][:],
                     in0=bkm[:, 0:1], scalar1=acols[k][:], scalar2=-1.0, op0=ALU.mult,
                     op1=ALU.mult))
        e3_ = b.emit("vector", lambda h, k=k: h.tensor_tensor(out=bcols2[k][:],
                     in0=bcols2[k][:], in1=gnbi[k][:], op=ALU.add))
        banks.set_free(bkm, e3_)
    t_cols = b.tok("vector")
    stS.close()
    b.barrier(); banks.reset()

    stO = contextlib.ExitStack()
    wo, two = cast_resident2(b, stO, tn("l1wo", [D, D], F32, "in"))
    bocols = []
    for k in range(NK):
        c3_, t_bo = load_col(b, stO, tn("l1bo", [NK, 128, 1], F32, "in")[k])
        bocols.append(c3_)
    ones_col = b.sb([128, 128], F32, stack=stO)
    t_ones = b.emit("vector", lambda h: h.memset(ones_col[:], 1.0))
    epsO = b.sb([128, 1], F32, stack=stO)
    b.emit("vector", lambda h: h.memset(epsO[:], 1e-5))
    otile = [b.sb([128, CW], F32, stack=stO) for _ in range(NK)]
    ogt = [b.sb([128, CW], BF16, stack=stO) for _ in range(NK)]
    ybg = [b.sb([128, CW], BF16, stack=stO) for _ in range(NK)]
    xf2 = [b.sb([128, CW], F32, stack=stO) for _ in range(NK)]
    res = [b.sb([128, CW], F32, stack=stO) for _ in range(NK)]
    sqt = b.sb([128, CW], F32, stack=stO)
    rows = b.sb([1, 4 * CW], F32, stack=stO)
    buf_free = None
    for n in range(NTOK):
        lo = []
        for k in range(NK):
            lo.append(b.emit("sync", lambda h, k=k, n=n: h.dma_start(out=otile[k][:],
                      in_=o_h[k * 128:(k + 1) * 128, n * CW:(n + 1) * CW]), dma=True,
                      waits=[buf_free]))
            lo.append(b.emit("sync", lambda h, k=k, n=n: h.dma_start(out=ogt[k][:],
                      in_=og_h[k * 128:(k + 1) * 128, n * CW:(n + 1) * CW]), dma=True))
            lo.append(b.emit("sync", lambda h, k=k, n=n: h.dma_start(out=xf2[k][:],
                      in_=x2T[k * 128:(k + 1) * 128, n * CW:(n + 1) * CW]), dma=True))
        gt = None
        for k in range(NK):
            g1 = b.emit("vector", lambda h, k=k: h.tensor_scalar(out=otile[k][:],
                        in0=otile[k][:], scalar1=acols[k][:], scalar2=bcols2[k][:],
                        op0=ALU.mult, op1=ALU.add), waits=[lo[3 * k], t_cols, buf_free])
            gt = b.emit("vector", lambda h, k=k: h.tensor_tensor(out=ybg[k][:],
                        in0=otile[k][:], in1=ogt[k][:], op=ALU.mult),
                        waits=[lo[3 * k + 1]])
        ev = None
        for m2 in range(NK):
            bank, bfree = banks.get()
            mmt = mm_chain(b, bank[:, 0:CW], [(wo[k][:, m2 * 128:(m2 + 1) * 128], ybg[k][:])
                                              for k in range(NK)],
                           waits=[two, gt], free_tok=bfree)
            ev = b.emit("vector", lambda h, bank=bank, m2=m2: h.scalar_tensor_tensor(
                out=res[m2][:], in0=bank[:, 0:CW], scalar=bocols[m2][:], in1=xf2[m2][:],
                op0=ALU.add, op1=ALU.add), waits=[mmt, t_bo, lo[-1]])
            banks.set_free(bank, ev)
        vt = ln_chunk(b, stO, banks, [r_[:] for r_ in res], CW, ones_col, epsO,
                      [r_[:] for r_ in res], affine=None, waits=[ev, t_ones], sq=sqt,
                      rows=rows)
        sts = []
        for m2 in range(NK):
            sts.append(b.emit("gpsimd", lambda h, m2=m2, n=n: h.dma_start(
                out=x3T[m2 * 128:(m2 + 1) * 128, n * CW:(n + 1) * CW], in_=res[m2][:]),
                dma=True, waits=[vt]))
        buf_free = [sts[-1], b.tok("tensor")]
    stO.close()
    b.barrier(); banks.reset()
    return b.tok("gpsimd", dma=True)


def _prep_l1_attn(inputs):
    f = np.float32
    c = _consts()
    d = {}
    for nm, key in (("l1wq", "Wq"), ("l1wk", "Wk"), ("l1wv", "Wv"), ("l1wg", "Wg"),
                    ("l1wo", "Wo")):
        d[nm] = np.ascontiguousarray(np.asarray(inputs["L1_attn_" + key], f))
    d["l1wa"] = np.ascontiguousarray(np.asarray(inputs["L1_attn_Wa"], f))
    d["l1wb"] = np.ascontiguousarray(np.asarray(inputs["L1_attn_Wb"], f))
    d["l1ba"] = np.asarray(inputs["L1_attn_ba"], f).reshape(16, 1).copy()
    d["l1bb"] = np.asarray(inputs["L1_attn_bb"], f).reshape(16, 1).copy()
    for nm, key in (("l1bq", "bq"), ("l1bk", "bk"), ("l1bv", "bv"), ("l1bg", "bg"),
                    ("l1bo", "bo"), ("l1gnsc", "gn_scale"), ("l1gnbi", "gn_bias")):
        d[nm] = np.ascontiguousarray(np.asarray(inputs["L1_attn_" + key], f).reshape(8, 128, 1))
    for nm, key in (("l1qc", "q_conv"), ("l1kc", "k_conv"), ("l1vc", "v_conv")):
        d[nm] = np.ascontiguousarray(np.asarray(inputs["L1_attn_" + key], f).reshape(8, 128, 4))
    d["cid"] = c["cid"]
    d["chead"] = np.ascontiguousarray(c["chead"].transpose(1, 0, 2).reshape(16, 1024))
    d["csel16"] = c["csel16"]
    d["celast"] = c["celast"]
    d["mUS"] = c["mUS"]
    d["mUI"] = c["mUI"]
    d["mLS"] = c["mLS"]
    return d


def kernel(**inputs):
    X = np.asarray(inputs["L0_attn_x"], np.float32)
    B = X.shape[0]
    cores = list(range(B))
    d0 = _prep_l0_attn(inputs)
    d1 = _prep_l0_ffn(inputs)
    d2 = _prep_l1_attn(inputs)
    d3 = _prep_l1_ffn(inputs)

    nc0, st0 = _mk_launch([stage_l0_attn], {"xT": "in", "x1T": "out"})
    in0 = [dict(d0, xT=np.ascontiguousarray(X[i].T)) for i in range(B)]
    r0 = run_bass_kernel_spmd(nc0, in0, core_ids=cores)
    st0.close()
    x1 = [np.ascontiguousarray(np.asarray(r0.results[i]["x1T"], np.float32)) for i in range(B)]

    nc1, st1 = _mk_launch([stage_l0_ffn], {"x1T": "in", "x2T": "out"})
    r1 = run_bass_kernel_spmd(nc1, [dict(d1, x1T=x1[i]) for i in range(B)], core_ids=cores)
    st1.close()
    x2 = [np.ascontiguousarray(np.asarray(r1.results[i]["x2T"], np.float32)) for i in range(B)]

    nc2, st2 = _mk_launch([stage_l1_attn], {"x2T": "in", "x3T": "out"})
    r2 = run_bass_kernel_spmd(nc2, [dict(d2, x2T=x2[i]) for i in range(B)], core_ids=cores)
    st2.close()
    x3 = [np.ascontiguousarray(np.asarray(r2.results[i]["x3T"], np.float32)) for i in range(B)]

    nc3, st3 = _mk_launch([stage_l1_ffn], {"x3T": "in", "outT": "out"})
    r3 = run_bass_kernel_spmd(nc3, [dict(d3, x3T=x3[i]) for i in range(B)], core_ids=cores)
    st3.close()
    out = np.stack([np.asarray(r3.results[i]["outT"], np.float32).T for i in range(B)])
    return out
